# version 06
# baseline (speedup 1.0000x reference)
import math
import numpy as np

DEBUG_SPILL = False
import ml_dtypes
import concourse.bass as bass
import concourse.mybir as mybir
from concourse.bass_utils import run_bass_kernel_spmd

F32 = mybir.dt.float32
BF16 = mybir.dt.bfloat16
I8 = mybir.dt.int8
AF = mybir.ActivationFunctionType
OP = mybir.AluOpType
AX = mybir.AxisListType
BF16NP = ml_dtypes.bfloat16

B, S, H = 4, 2048, 4096
NH, DN, DR, DV = 32, 128, 64, 128
QL, KVL = 1536, 512
INTER = 11008
NIC = INTER // 128
NFC = H // 128
NKC_QL = QL // 128
NCC = KVL // 128
EPS_RMS, EPS_LN = 1e-6, 1e-5
SCALE = 1.0 / math.sqrt(DN + DR)
TQ = 1024
NSLOT = 8
MASKVAL = -1e30

QTILES = [[0, 3, 4, 7, 8, 11, 12, 15], [1, 2, 5, 6, 9, 10, 13, 14]]
KS = [2 * (s + 1) for s in range(NSLOT)]

ENGINES = ("tensor", "vector", "scalar", "gpsimd", "sync")


class Sem:
    def __init__(self, name):
        self.name = name
        self.n = 0
        self.h = None


class Emitter:
    def __init__(self):
        self.ops = {e: [] for e in ENGINES}

    def emit(self, eng, fn, wait=(), inc=None, step=1):
        w = [(s, v) for (s, v) in wait if s is not None and v and v > 0]
        tok = None
        if inc is not None:
            inc.n += step
            tok = inc.n
        self.ops[eng].append((fn, w, (inc, step) if inc is not None else None))
        return tok

    def barrier(self, sems):
        snap = [(s, s.n) for s in sems if s.n > 0]
        for e in ENGINES:
            self.ops[e].append((None, snap, None))

    def replay(self, eng_name, eng):
        waited = {}
        for fn, waits, inc in self.ops[eng_name]:
            for sem, val in waits:
                if waited.get(sem, -1) < val:
                    eng.wait_ge(sem.h, val)
                    waited[sem] = val
            if fn is not None:
                ins = fn(eng)
                if inc is not None:
                    ins.then_inc(inc[0].h, inc[1])


def _w(*tokens):
    return [t for t in tokens if t is not None]


def build_program():
    nc = bass.Bass("TRN2", target_bir_lowering=False)
    E = Emitter()
    PE, VE, SE = Sem("PE"), Sem("VE"), Sem("SE")
    emit = E.emit

    d = {}
    def din(name, shape, dt):
        d[name] = nc.dram_tensor(name, shape, dt, kind="ExternalInput").ap()
    din("xT_all", [H, S], F32)
    din("xT_own", [H, TQ], F32)
    din("wqa_t", [NKC_QL, 128, NFC, 128], F32)
    din("wqb_t", [NH, 128, NKC_QL, 192], F32)
    din("wkva_c", [NCC, 128, NFC, 128], F32)
    din("wkva_pe", [128, NFC, 64], F32)
    din("wkvb_k", [NH, 128, NCC, 128], F32)
    din("wkvb_v", [NCC, 128, NH, 128], F32)
    din("wo_t", [NFC, 128, NFC, 128], F32)
    din("wg_t", [NIC, 128, NFC, 128], I8)
    din("wu_t", [NIC, 128, NFC, 128], I8)
    din("wd_t", [NFC, 128, NIC, 128], I8)
    din("qnw_t", [128, NKC_QL], F32)
    din("kvnw_t", [128, NCC], F32)
    din("sg_t", [128, NIC], F32)
    din("su_t", [128, NIC], F32)
    din("sd_t", [128, NFC], F32)
    din("ident", [128, 128], BF16)
    din("onesb", [128, 1], BF16)
    din("ones1", [1, 128], F32)
    din("mask2", [128, NSLOT * 2 * 128], BF16)
    din("ropeC", [128, TQ], F32)
    din("ropeS", [128, TQ], F32)
    din("coskp", [64, S], F32)
    din("sinkp", [64, S], F32)

    sck = "ExternalOutput" if DEBUG_SPILL else "Internal"
    outT = nc.dram_tensor("outT", [H, TQ], F32, kind="ExternalOutput").ap()
    attn_sc = nc.dram_tensor("attn_sc", [H, TQ], BF16, kind=sck).ap()
    xres_sc = nc.dram_tensor("xres_sc", [H, TQ], F32, kind=sck).ap()
    y_sc = nc.dram_tensor("y_sc", [H, TQ], F32, kind=sck).ap()
    ybf_sc = nc.dram_tensor("ybf_sc", [H, TQ], BF16, kind=sck).ap()
    if DEBUG_SPILL:
        dbg_qn = nc.dram_tensor("dbg_qn", [128, NKC_QL * TQ], BF16, kind="ExternalOutput").ap()
        dbg_kvn = nc.dram_tensor("dbg_kvn", [128, NCC * S], BF16, kind="ExternalOutput").ap()
        dbg_kpe = nc.dram_tensor("dbg_kpe", [64, S], BF16, kind="ExternalOutput").ap()
        dbg_h = nc.dram_tensor("dbg_h", [2, 128, NIC * 512], BF16, kind="ExternalOutput").ap()
        dbg_x1 = nc.dram_tensor("dbg_x1", [H, TQ], F32, kind="ExternalOutput").ap()
        dbg_sg = nc.dram_tensor("dbg_sg", [2, 128, 512], BF16, kind="ExternalOutput").ap()
        dbg_us = nc.dram_tensor("dbg_us", [2, 128, 512], BF16, kind="ExternalOutput").ap()

    import contextlib
    with contextlib.ExitStack() as stack:
        ec = stack.enter_context
        ident_sb = ec(nc.sbuf_tensor("ident_sb", [128, 128], BF16))
        onesb_sb = ec(nc.sbuf_tensor("onesb_sb", [128, 1], BF16))
        ones1_sb = ec(nc.sbuf_tensor("ones1_sb", [1, 128], F32))
        qnw_sb = ec(nc.sbuf_tensor("qnw_sb", [128, NKC_QL], F32))
        kvnw_sb = ec(nc.sbuf_tensor("kvnw_sb", [128, NCC], F32))
        sg_sb = ec(nc.sbuf_tensor("sg_sb", [128, NIC], F32))
        su_sb = ec(nc.sbuf_tensor("su_sb", [128, NIC], F32))
        sd_sb = ec(nc.sbuf_tensor("sd_sb", [128, NFC], F32))
        rows = ec(nc.sbuf_tensor("rows", [1, 4096], F32))
        zbuf = ec(nc.sbuf_tensor("zbuf", [128, 64], F32))
        pbf = ec(nc.sbuf_tensor("pbf", [128, 73472], BF16))
        pf32 = ec(nc.sbuf_tensor("pf32", [128, 7168], F32))
        PS_SC = ec(nc.psum_tensor("PS_SC", [128, 2048], F32))
        PS_PTA = ec(nc.psum_tensor("PS_PTA", [128, 1024], BF16))
        PS_PTB = ec(nc.psum_tensor("PS_PTB", [128, 1024], BF16))
        PS_A = ec(nc.psum_tensor("PS_A", [128, 512], F32))
        PS_B = ec(nc.psum_tensor("PS_B", [128, 512], F32))
        PE.h = ec(nc.semaphore("sPE"))
        VE.h = ec(nc.semaphore("sVE"))
        SE.h = ec(nc.semaphore("sSE"))
        block = ec(nc.Block())

        sems = {}
        def SM(name):
            if name not in sems:
                s_ = Sem(name)
                s_.h = ec(nc.semaphore("sm_" + name))
                sems[name] = s_
            return sems[name]
        def allsems():
            return [PE, VE, SE] + list(sems.values())

        def bfv(col, n):
            return pbf[:, col:col + n]
        def f32v(col, n):
            return pf32[:, col:col + n]
        def rowt(r, n):
            return rows[0:1, r * 1024:r * 1024 + n]

        qn = bfv(45568, NKC_QL * TQ)
        kvn = bfv(57856, NCC * S)
        kpe = bfv(66048, S)[0:64, :]
        mask2_sb = bfv(68096, NSLOT * 2 * 128)

        for (dst, src) in [
            (ident_sb[:], d["ident"]), (onesb_sb[:], d["onesb"]), (ones1_sb[:], d["ones1"]),
            (qnw_sb[:], d["qnw_t"]), (kvnw_sb[:], d["kvnw_t"]),
            (sg_sb[:], d["sg_t"]), (su_sb[:], d["su_t"]), (sd_sb[:], d["sd_t"]),
            (mask2_sb[:], d["mask2"]),
        ]:
            emit("sync", lambda e, a=dst, b=src: e.dma_start(out=a, in_=b), inc=SM("consts"), step=16)
        tok_consts = (SM("consts"), SM("consts").n)

        xab = [bfv(0, 16384), bfv(25088, 16384)]
        aslab = [bfv(16384, 4096), bfv(20480, 4096), bfv(41472, 4096)]
        sqst = bfv(24576, 512)
        cosk = pf32[0:64, 2048:4096]
        sink = pf32[0:64, 4096:6144]
        krA = pf32[:, 6144:6656]
        krB = pf32[:, 6656:7168]

        emit("sync", lambda e: e.dma_start(out=cosk, in_=d["coskp"]), inc=SM("consts"), step=16)
        emit("sync", lambda e: e.dma_start(out=sink, in_=d["sinkp"]), inc=SM("consts"), step=16)
        tok_consts = (SM("consts"), SM("consts").n)

        use_xa = [None, None]
        use_slab = [None, None, None]
        use_sq = None
        ps_free = {}

        slab_jobs = []
        for _q in range(4):
            for _mc in range(NCC):
                slab_jobs.append((d["wkva_c"][_mc], 128))
            slab_jobs.append((d["wkva_pe"], 64))
        for _q in range(2):
            for _mc in range(NKC_QL):
                slab_jobs.append((d["wqa_t"][_mc], 128))
        slab_tok = [None] * len(slab_jobs)
        slab_emitted = [0]

        def slab_emit_next():
            j = slab_emitted[0]
            if j >= len(slab_jobs):
                return
            b = j % 3
            srcj, mw = slab_jobs[j]
            sm = SM("aslab%d" % b)
            slab_tok[j] = (sm, emit("gpsimd", lambda e, s_=aslab[b], a=srcj, m_=mw: e.dma_start(
                out=s_.rearrange("p (kc m) -> p kc m", kc=NFC)[:, :, 0:m_], in_=a),
                wait=_w(use_slab[b]), inc=sm, step=16))
            slab_emitted[0] += 1

        def proj_quarter(xa, tok_xa, nmc, dst_cols_fn, nw_sb, nfeat, pe_chunk, jbase, q4):
            nonlocal use_sq
            for mc in range(nmc + (1 if pe_chunk else 0)):
                is_pe = pe_chunk and mc == nmc
                j = jbase + mc
                sl = j % 3
                slab = aslab[sl]
                mwid = 64 if is_pe else 128
                while slab_emitted[0] <= j:
                    slab_emit_next()
                tok_slab = slab_tok[j]
                if is_pe:
                    ps = PS_SC[0:64, 1024:1536]
                    pskey = 2
                else:
                    ps = PS_SC[:, sl * 512:(sl + 1) * 512]
                    pskey = sl
                wmm = _w(ps_free.get(pskey), tok_xa, tok_slab)
                tmm = None
                for fc in range(NFC):
                    tmm = emit("tensor", lambda e, p=ps, s_=slab, f=fc, mw=mwid,
                               st=(fc == 0), sp=(fc == NFC - 1):
                               e.matmul(p, s_[:, f * 128:f * 128 + mw],
                                        xa[:, f * 512:(f + 1) * 512], start=st, stop=sp),
                               wait=(wmm if fc == 0 else ()),
                               inc=(PE if fc == NFC - 1 else None))
                use_slab[sl] = (PE, tmm)
                slab_emit_next()
                if not is_pe:
                    st_f32 = f32v(mc * 512, 512)
                    tok_ev = emit("vector", lambda e, o=st_f32, p=ps: e.tensor_copy(o, p),
                                  wait=[(PE, tmm)], inc=VE)
                    ps_free[pskey] = (VE, tok_ev)
                    tok_sq = emit("scalar", lambda e, i=st_f32: e.activation(sqst[:], i, AF.Square),
                                  wait=_w((VE, tok_ev), use_sq), inc=SE)
                    tok_ss = emit("tensor", lambda e, st=(mc == 0), sp=(mc == nmc - 1):
                                  e.matmul(PS_A[0:1, :], onesb_sb[:], sqst[:], start=st, stop=sp),
                                  wait=[(SE, tok_sq)] + (_w(ps_free.get("ss")) if mc == 0 else []),
                                  inc=PE)
                    use_sq = (PE, tok_ss)
                else:
                    tA = emit("vector", lambda e: e.tensor_copy(krA[0:64, :], PS_SC[0:64, 1024:1536]),
                              wait=[(PE, tmm)], inc=VE)
                    ps_free[pskey] = (VE, tA)
                    tb1 = emit("scalar", lambda e: e.copy(krB[0:32, :], krA[32:64, :]),
                               wait=[(VE, tA)], inc=SE)
                    tb2 = emit("scalar", lambda e: e.copy(krB[32:64, :], krA[0:32, :]),
                               wait=[(VE, tA)], inc=SE)
                    ck = cosk[:, q4 * 512:(q4 + 1) * 512]
                    sk = sink[:, q4 * 512:(q4 + 1) * 512]
                    emit("vector", lambda e, c=ck: e.tensor_mul(krA[64:128, :], krA[0:64, :], c),
                         wait=_w((SE, tb2), tok_consts), inc=VE)
                    emit("vector", lambda e, s_=sk: e.tensor_mul(krB[64:128, :], krB[0:64, :], s_),
                         inc=VE)
                    emit("vector", lambda e, c=q4: e.tensor_add(
                        kpe[:, c * 512:(c + 1) * 512], krA[64:128, :], krB[64:128, :]), inc=VE)
            tr1 = emit("vector", lambda e, nf=nfeat: e.tensor_scalar(
                rowt(0, 512), PS_A[0:1, :], 1.0 / nf, EPS_RMS, OP.mult, OP.add),
                wait=[(PE, PE.n)], inc=VE)
            ps_free["ss"] = (VE, tr1)
            emit("vector", lambda e: e.reciprocal(rowt(1, 512), rowt(0, 512)), inc=VE)
            tr3 = emit("scalar", lambda e: e.activation(rowt(2, 512), rowt(1, 512), AF.Sqrt),
                       wait=[(VE, VE.n)], inc=SE)
            tbc = emit("tensor", lambda e: e.matmul(PS_SC[:, 1536:2048], ones1_sb[:], rowt(2, 512),
                                                    start=True, stop=True),
                       wait=[(SE, tr3)] + _w(ps_free.get(3)), inc=PE)
            tkn = None
            for mc in range(nmc):
                tkn = emit("vector", lambda e, m=mc: e.scalar_tensor_tensor(
                    dst_cols_fn(m), f32v(m * 512, 512), nw_sb[:, m:m + 1], PS_SC[:, 1536:2048],
                    OP.mult, OP.mult),
                    wait=[(PE, tbc)], inc=VE)
            ps_free[3] = (VE, tkn)

        def a_src(qi):
            if qi < 4:
                return d["xT_all"].rearrange("(fc p) t -> p fc t", p=128)[
                    :, :, qi * 512:(qi + 1) * 512]
            oq = qi - 4
            return d["xT_own"].rearrange("(fc p) t -> p fc t", p=128)[
                :, :, oq * 512:(oq + 1) * 512]

        tok_xa_ld = [None] * 6
        def a_load(qi):
            sm = SM("xa%d" % (qi % 2))
            t_ = emit("gpsimd", lambda e, a=a_src(qi), x=xab[qi % 2]: e.dma_start(
                out=x[:].rearrange("p (fc t) -> p fc t", fc=NFC), in_=a),
                wait=_w(use_xa[qi % 2]), inc=sm, step=16)
            tok_xa_ld[qi] = (sm, t_)

        a_load(0)
        a_load(1)
        slab_emit_next()
        slab_emit_next()
        for qi in range(6):
            if qi < 4:
                proj_quarter(xab[qi % 2], tok_xa_ld[qi], NCC,
                             (lambda m, c=qi: kvn[:, m * S + c * 512:m * S + c * 512 + 512]),
                             kvnw_sb, KVL, True, qi * (NCC + 1), qi)
            else:
                oq = qi - 4
                proj_quarter(xab[qi % 2], tok_xa_ld[qi], NKC_QL,
                             (lambda m, c=oq: qn[:, m * TQ + c * 512:m * TQ + c * 512 + 512]),
                             qnw_sb, QL, False, 4 * (NCC + 1) + oq * NKC_QL, 0)
            use_xa[qi % 2] = (PE, PE.n)
            if qi + 2 < 6:
                a_load(qi + 2)

        if DEBUG_SPILL:
            emit("sync", lambda e: e.dma_start(out=dbg_qn, in_=qn), wait=[(VE, VE.n)],
                 inc=SM("dbg"), step=16)
            emit("sync", lambda e: e.dma_start(out=dbg_kvn, in_=kvn), inc=SM("dbg"), step=16)
            emit("sync", lambda e: e.dma_start(out=dbg_kpe, in_=kpe), inc=SM("dbg"), step=16)

        E.barrier(allsems())

        kslab = [bfv(0, 2048), bfv(2048, 2048)]
        vw = [bfv(4096, 2048), bfv(6144, 2048)]
        qbs = [bfv(8192, 2304), bfv(10496, 2304)]
        knope = bfv(12800, 4 * 2048)
        vbuf = bfv(20992, 16 * 512)
        qnope = [bfv(29184, 1024), bfv(30208, 1024)]
        qpe = [bfv(31232, 1024)[0:64, :], bfv(32256, 1024)[0:64, :]]
        Pt = bfv(33280, 2048)
        Pn = bfv(35328, 2048)
        PT = bfv(37376, 2048)
        attn_h = [bfv(39424, 1024), bfv(40448, 1024)]
        ropeA = f32v(0, 1024)
        ropeB = f32v(1024, 1024)
        ropeT1 = f32v(2048, 1024)
        ropeT2 = f32v(3072, 1024)
        ropeC_v = f32v(4096, 1024)
        ropeS_v = f32v(5120, 1024)

        emit("sync", lambda e: e.dma_start(out=ropeC_v, in_=d["ropeC"]), inc=SM("consts"), step=16)
        emit("sync", lambda e: e.dma_start(out=ropeS_v, in_=d["ropeS"]), inc=SM("consts"), step=16)
        tok_bconsts = (SM("consts"), SM("consts").n)

        use_group = [None, None]
        use_qbs = [None, None]
        use_attn = [None, None]
        pend = {}
        tok_tr_pe_prev = None

        def flush_A():
            if not pend or pend.get("pn") is not None:
                return
            z, n = pend["zcol"], pend["ncols"]
            trz = emit("vector", lambda e, z_=z: e.reciprocal(
                zbuf[:, 16 + z_:17 + z_], zbuf[:, z_:z_ + 1]),
                wait=[(SE, pend["exp"])], inc=VE)
            pend["pn"] = emit("vector", lambda e, n_=n, z_=z: e.tensor_scalar_mul(
                Pn[:, 0:n_], Pt[:, 0:n_], zbuf[:, 16 + z_:17 + z_]),
                wait=_w((VE, trz), tok_tr_pe_prev), inc=VE)

        def flush_B():
            nonlocal tok_tr_pe_prev
            if not pend:
                return
            flush_A()
            s, Ks, h4, i = pend["s"], pend["Ks"], pend["h4"], pend["i"]
            nwave = (Ks + 7) // 8
            tok_tr_evac = None
            for wv_ in range(nwave):
                ptps = PS_PTA if wv_ % 2 == 0 else PS_PTB
                wlen = min(8, Ks - wv_ * 8)
                ttr = None
                for j in range(wlen):
                    tc = wv_ * 8 + j
                    ttr = emit("tensor", lambda e, p=ptps, jj=j, t=tc:
                               e.transpose(p[:, jj * 128:(jj + 1) * 128],
                                           Pn[:, t * 128:(t + 1) * 128], ident_sb[:]),
                               wait=(_w((VE, pend["pn"]), tok_pt_evac[wv_ % 2])
                                     if j == 0 else ()),
                               inc=(PE if j == wlen - 1 else None))
                tev = emit("vector", lambda e, p=ptps, w_=wv_, l=wlen: e.tensor_copy(
                    PT[:, w_ * 1024:w_ * 1024 + l * 128], p[:, 0:l * 128]),
                    wait=[(PE, ttr)], inc=VE)
                tok_pt_evac[wv_ % 2] = (VE, tev)
                tok_tr_evac = (VE, tev)
            tok_tr_pe_prev = (PE, PE.n)
            psv_i = s % 2
            psv = PS_A if psv_i == 0 else PS_B
            tmm = None
            for tc in range(Ks):
                tmm = emit("tensor", lambda e, p=psv, t=tc, h4_=h4,
                           st=(tc == 0), sp=(tc == Ks - 1):
                           e.matmul(p[:, 0:128],
                                    vbuf[:, t * 512 + h4_ * 128:t * 512 + (h4_ + 1) * 128],
                                    PT[:, t * 128:(t + 1) * 128], start=st, stop=sp),
                           wait=(_w(tok_tr_evac, tok_av_evac[psv_i], tok_v_ready)
                                 if tc == 0 else ()),
                           inc=(PE if tc == Ks - 1 else None))
            tev = emit("scalar", lambda e, p=psv, ss=s, a=pend["attn"]: e.copy(
                a[:, ss * 128:(ss + 1) * 128], p[:, 0:128]),
                wait=[(PE, tmm)] + (_w(use_attn[i]) if s == 0 else []),
                inc=SE)
            tok_av_evac[psv_i] = (SE, tev)
            if s == NSLOT - 1:
                sm = SM("ast%d" % i)
                tst = emit("sync", lambda e, a=pend["attn"], h=pend["hh"]: e.dma_start(
                    out=attn_sc[h * 128:(h + 1) * 128, :], in_=a),
                    wait=[(SE, SE.n)], inc=sm, step=16)
                use_attn[i] = (sm, tst)
            pend.clear()
        tok_av_evac = {0: None, 1: None}
        tok_pt_evac = {0: None, 1: None}
        tok_knope_ready = None
        tok_v_ready = None
        tok_scores_free = None
        tok_qn_evac = [None, None]
        tok_rope_done = None

        for pair in range(16):
            he, ho = 2 * pair, 2 * pair + 1
            if pair % 2 == 0:
                flush_B()
                g4 = pair // 2
                sl = g4 % 2
                smk, smv = SM("kv%d" % sl), SM("vw%d" % sl)
                tok_k = (smk, emit("gpsimd", lambda e, s_=kslab[sl], g=g4: e.dma_start(
                    out=s_.rearrange("p (h cc d) -> p h cc d", h=4, cc=NCC),
                    in_=d["wkvb_k"][4 * g:4 * g + 4].rearrange("h p cc d -> p h cc d")),
                    wait=_w(use_group[sl]), inc=smk, step=16))
                tok_vw = (smv, emit("gpsimd", lambda e, s_=vw[sl], g=g4: e.dma_start(
                    out=s_.rearrange("p (cc hd) -> p cc hd", cc=NCC),
                    in_=d["wkvb_v"][:, :, 4 * g:4 * g + 4, :].rearrange("cc p h d -> p cc (h d)")),
                    wait=_w(use_group[sl]), inc=smv, step=16))
                wv = _w(use_group[0], use_group[1])
                for tc in range(16):
                    psv = PS_A if tc % 2 == 0 else PS_B
                    wk = _w(tok_vw, tok_av_evac[tc % 2]) + (wv if tc < 2 else [])
                    tmm = None
                    for cc in range(NCC):
                        tmm = emit("tensor", lambda e, p=psv, c=cc, t=tc, s_=vw[sl],
                                   st=(cc == 0), sp=(cc == NCC - 1):
                                   e.matmul(p[:, 0:512], kvn[:, c * S + t * 128:c * S + t * 128 + 128],
                                            s_[:, c * 512:(c + 1) * 512], start=st, stop=sp),
                                   wait=(wk if cc == 0 else ()),
                                   inc=(PE if cc == NCC - 1 else None))
                    tev = emit("vector", lambda e, p=psv, t=tc: e.tensor_copy(
                        vbuf[:, t * 512:(t + 1) * 512], p[:, 0:512]),
                        wait=[(PE, tmm)], inc=VE)
                    tok_av_evac[tc % 2] = (VE, tev)
                tok_v_ready = (VE, VE.n)
                for hh4 in range(4):
                    for nb in range(4):
                        psv = PS_A if nb % 2 == 0 else PS_B
                        wk = _w(tok_k, tok_av_evac[nb % 2])
                        tmm = None
                        for cc in range(NCC):
                            tmm = emit("tensor", lambda e, p=psv, h=hh4, c=cc, n=nb, s_=kslab[sl],
                                       st=(cc == 0), sp=(cc == NCC - 1):
                                       e.matmul(p[:, 0:512],
                                                s_[:, (h * NCC + c) * 128:(h * NCC + c) * 128 + 128],
                                                kvn[:, c * S + n * 512:c * S + n * 512 + 512],
                                                start=st, stop=sp),
                                       wait=(wk if cc == 0 else ()),
                                       inc=(PE if cc == NCC - 1 else None))
                        tev = emit("vector", lambda e, p=psv, h=hh4, n=nb: e.tensor_copy(
                            knope[:, h * 2048 + n * 512:h * 2048 + n * 512 + 512], p[:, 0:512]),
                            wait=[(PE, tmm)], inc=VE)
                        tok_av_evac[nb % 2] = (VE, tev)
                tok_knope_ready = (VE, VE.n)

            tok_qbs_ld = [None, None]
            for i, hh in enumerate((he, ho)):
                sm = SM("qbs%d" % i)
                tok_qbs_ld[i] = (sm, emit("gpsimd", lambda e, s_=qbs[i], h=hh: e.dma_start(
                    out=s_.rearrange("p (kc m) -> p kc m", kc=NKC_QL), in_=d["wqb_t"][h]),
                    wait=_w(use_qbs[i]), inc=sm, step=16))

            for i, hh in enumerate((he, ho)):
                wq = _w(tok_qbs_ld[i], tok_scores_free,
                        tok_qn_evac[0], tok_qn_evac[1])
                tmm = None
                for nb in range(2):
                    for kc in range(NKC_QL):
                        tmm = emit("tensor", lambda e, n=nb, k=kc, s_=qbs[i],
                                   st=(kc == 0), sp=(kc == NKC_QL - 1):
                                   e.matmul(PS_SC[:, n * 512:(n + 1) * 512],
                                            s_[:, k * 192:k * 192 + 128],
                                            qn[:, k * TQ + n * 512:k * TQ + n * 512 + 512],
                                            start=st, stop=sp),
                                   wait=(wq if (nb == 0 and kc == 0) else ()),
                                   inc=(PE if (nb == 1 and kc == NKC_QL - 1) else None))
                tev = emit("scalar", lambda e, o=qnope[i]: e.activation(
                    o, PS_SC[:, 0:1024], AF.Copy, scale=SCALE),
                    wait=[(PE, tmm)], inc=SE)
                tok_qn_evac[i] = (SE, tev)
                wr = _w(tok_rope_done, tok_scores_free) if i == 0 else [(VE, VE.n)]
                tmm = None
                for nb in range(2):
                    for kc in range(NKC_QL):
                        tmm = emit("tensor", lambda e, n=nb, k=kc, s_=qbs[i],
                                   st=(kc == 0), sp=(kc == NKC_QL - 1):
                                   e.matmul(PS_SC[0:64, 1024 + n * 512:1024 + (n + 1) * 512],
                                            s_[:, k * 192 + 128:k * 192 + 192],
                                            qn[:, k * TQ + n * 512:k * TQ + n * 512 + 512],
                                            start=st, stop=sp),
                                   wait=(wr if (nb == 0 and kc == 0) else ()),
                                   inc=(PE if (nb == 1 and kc == NKC_QL - 1) else None))
                use_qbs[i] = (PE, tmm)
                base = 0 if i == 0 else 64
                trv = emit("vector", lambda e, b_=base: e.tensor_copy(
                    ropeA[b_:b_ + 64, :], PS_SC[0:64, 1024:2048]),
                    wait=[(PE, tmm)], inc=VE)
                tok_rope_done = (VE, trv)
            tb = None
            for (db, sb_) in [(0, 32), (32, 0), (64, 96), (96, 64)]:
                tb = emit("scalar", lambda e, a=db, b_=sb_: e.copy(
                    ropeB[a:a + 32, :], ropeA[b_:b_ + 32, :]),
                    wait=[(VE, tok_rope_done[1])], inc=SE)
            emit("vector", lambda e: e.tensor_mul(ropeT1[:], ropeA[:], ropeC_v[:]),
                 wait=_w((SE, tb), tok_bconsts), inc=VE)
            emit("vector", lambda e: e.tensor_mul(ropeT2[:], ropeB[:], ropeS_v[:]), inc=VE)
            emit("vector", lambda e: e.tensor_add(qpe[0][:, :], ropeT1[0:64, :], ropeT2[0:64, :]),
                 inc=VE)
            tq2 = emit("vector", lambda e: e.tensor_add(
                qpe[1][:, :], ropeT1[64:128, :], ropeT2[64:128, :]), inc=VE)
            tok_qpe = (VE, tq2)

            for i, hh in enumerate((he, ho)):
                h4 = hh % 4
                for s in range(NSLOT):
                    Ks = KS[s]
                    ncols = Ks * 128
                    nbk = (ncols + 511) // 512
                    wsc = _w(tok_scores_free, tok_qn_evac[i], tok_qpe, tok_knope_ready,
                             tok_consts)
                    first = True
                    for nb in range(nbk):
                        c0 = nb * 512
                        c1 = min(ncols, c0 + 512)
                        emit("tensor", lambda e, a=c0, b_=c1, q=qnope[i], h4_=h4, ss=s:
                             e.matmul(PS_SC[:, a:b_], q[:, ss * 128:(ss + 1) * 128],
                                      knope[:, h4_ * 2048 + a:h4_ * 2048 + b_],
                                      start=True, stop=False),
                             wait=(wsc if first else ()))
                        first = False
                        emit("tensor", lambda e, a=c0, b_=c1, qp=qpe[i], ss=s,
                             sp=(c1 < ncols):
                             e.matmul(PS_SC[:, a:b_], qp[:, ss * 128:(ss + 1) * 128],
                                      kpe[:, a:b_], start=False, stop=sp))
                    tok_sc = None
                    for j in range(2):
                        c0 = (Ks - 2 + j) * 128
                        tok_sc = emit("tensor", lambda e, a=c0, ss=s, jj=j, sp=(j == 1):
                                      e.matmul(PS_SC[:, a:a + 128], ident_sb[:],
                                               mask2_sb[:, (ss * 2 + jj) * 128:(ss * 2 + jj + 1) * 128],
                                               start=False, stop=sp),
                                      inc=(PE if j == 1 else None))
                    flush_A()
                    zcol = i * 8 + s
                    tmax = emit("vector", lambda e, n=ncols, z=zcol: e.reduce_max(
                        zbuf[:, 32 + z:33 + z], PS_SC[:, 0:n], AX.X, negate=True),
                        wait=_w((PE, tok_sc), tok_scores_free), inc=VE)
                    texp = emit("scalar", lambda e, n=ncols, z=zcol: e.activation(
                        Pt[:, 0:n], PS_SC[:, 0:n], AF.Exp, bias=zbuf[:, 32 + z:33 + z],
                        accum_out=zbuf[:, z:z + 1]),
                        wait=[(VE, tmax)], inc=SE)
                    tok_scores_free = (SE, texp)
                    flush_B()
                    pend.update(i=i, hh=hh, h4=h4, s=s, Ks=Ks, ncols=ncols, zcol=zcol,
                                exp=texp, pn=None, attn=attn_h[i])
            use_group[(ho // 4) % 2] = (PE, PE.n)
        flush_B()
        tok_bend_pe = (PE, PE.n)
        tok_bend_ve = (VE, VE.n)
        tok_bend_se = (SE, SE.n)
        tok_ast = [(SM("ast0"), SM("ast0").n), (SM("ast1"), SM("ast1").n)]

        rhsb = [bfv(0, 16384), bfv(16384, 16384)]
        wos = [bfv(32768, 4096), bfv(36864, 4096)]
        xbf_st = [bfv(40960, 512), bfv(41472, 512)]
        sq_st = [bfv(41984, 512), bfv(42496, 512)]
        p2yb = [bfv(43008, 512), bfv(43520, 512), bfv(44544, 512)]
        x32b = [f32v(0, 512), f32v(512, 512)]
        xres_st = [f32v(1024, 512), f32v(1536, 512)]
        p2x = [f32v(2048, 512), f32v(2560, 512), f32v(3072, 512)]
        p2t = [f32v(3584, 512), f32v(4096, 512), f32v(4608, 512)]

        use_wos = [None, None]
        use_x32 = [None, None]
        use_xres = [None, None]
        use_ld = [None, None, None]
        use_y = [None, None, None]
        use_ysc = [None, None, None]
        use_tyb = [None, None, None]
        tok_wo_free = [None, None]
        tok_stat_pe = [None, None]
        tok_rhs_t = [None, None]
        for c_ in range(2):
            smr = SM("rhs%d" % c_)
            for g in range(2):
                emit("sync", lambda e, gg=g, c=c_: e.dma_start(
                    out=rhsb[c][:, gg * 8192:(gg + 1) * 8192].rearrange("p (kc t) -> p kc t", kc=16),
                    in_=attn_sc.rearrange("(kc p) t -> p kc t", p=128)[
                        :, gg * 16:(gg + 1) * 16, c * 512:(c + 1) * 512]),
                    wait=(_w(tok_bend_pe, tok_bend_ve, *tok_ast) if g == 0 else ()),
                    inc=smr, step=16)
            tok_rhs_t[c_] = (smr, smr.n)
        for ct in range(2):
            rhs = rhsb[ct]
            tok_rhs = tok_rhs_t[ct]

            pend_stats = None
            for fc in range(NFC):
                sl = fc % 2
                smw, smx = SM("wo%d" % sl), SM("x32%d" % sl)
                first2 = (ct == 0 and fc < 2)
                tok_ws = (smw, emit("gpsimd", lambda e, s_=wos[sl], f=fc: e.dma_start(
                    out=s_.rearrange("p (kc m) -> p kc m", kc=NFC), in_=d["wo_t"][f]),
                    wait=_w(use_wos[sl]) + (_w(tok_bend_pe, tok_bend_ve, *tok_ast)
                                            if first2 else []),
                    inc=smw, step=16))
                tok_x32 = (smx, emit("sync", lambda e, b_=x32b[sl], f=fc, c=ct: e.dma_start(
                    out=b_, in_=d["xT_own"][f * 128:(f + 1) * 128, c * 512:(c + 1) * 512]),
                    wait=_w(use_x32[sl]) + (_w(tok_bend_ve, tok_bend_se) if first2 else []),
                    inc=smx, step=16))
                ps = PS_SC[:, sl * 512:(sl + 1) * 512]
                wmm = _w(tok_ws, tok_rhs, tok_wo_free[sl]) + (
                    _w(tok_scores_free) if first2 else [])
                tmm = None
                for kc in range(NFC):
                    tmm = emit("tensor", lambda e, p=ps, k=kc, s_=wos[sl], r_=rhs,
                               st=(kc == 0), sp=(kc == NFC - 1):
                               e.matmul(p, s_[:, k * 128:(k + 1) * 128],
                                        r_[:, k * 512:(k + 1) * 512], start=st, stop=sp),
                               wait=(wmm if kc == 0 else ()),
                               inc=(PE if kc == NFC - 1 else None))
                use_wos[sl] = (PE, tmm)
                if pend_stats is not None:
                    psl, ptbf, ptsq, pfc = pend_stats
                    wst = _w(tok_bend_se) if (ct == 0 and pfc == 0) else []
                    emit("tensor", lambda e, l_=psl, st=(pfc == 0): e.matmul(
                        PS_A[0:1, :], onesb_sb[:], xbf_st[l_][:], start=st, stop=False),
                        wait=[(VE, ptbf)] + wst)
                    tss = emit("tensor", lambda e, l_=psl, st=(pfc == 0): e.matmul(
                        PS_B[0:1, :], onesb_sb[:], sq_st[l_][:], start=st, stop=False),
                        wait=[(SE, ptsq)] + wst, inc=PE)
                    tok_stat_pe[psl] = (PE, tss)
                tres = emit("vector", lambda e, o=xres_st[sl], p=ps, b_=x32b[sl]: e.tensor_add(
                    o, p, b_),
                    wait=_w((PE, tmm), tok_x32, use_xres[sl]), inc=VE)
                tok_wo_free[sl] = (VE, tres)
                use_x32[sl] = (VE, tres)
                tbf = emit("vector", lambda e, i=xres_st[sl], o=xbf_st[sl]: e.tensor_copy(o[:], i),
                           wait=_w(tok_stat_pe[sl]) + (_w(*tok_ast) if first2 else []),
                           inc=VE)
                tsq = emit("scalar", lambda e, i=xres_st[sl], o=sq_st[sl]: e.activation(
                    o[:], i, AF.Square),
                    wait=_w((VE, tres), tok_stat_pe[sl]), inc=SE)
                pend_stats = (sl, tbf, tsq, fc)
                smxs = SM("xst%d" % sl)
                tstx = emit("sync", lambda e, i=xres_st[sl], f=fc, c=ct: e.dma_start(
                    out=xres_sc[f * 128:(f + 1) * 128, c * 512:(c + 1) * 512], in_=i),
                    wait=[(VE, tres)], inc=smxs, step=16)
                if DEBUG_SPILL:
                    tstx = emit("sync", lambda e, i=xres_st[sl], f=fc, c=ct: e.dma_start(
                        out=dbg_x1[f * 128:(f + 1) * 128, c * 512:(c + 1) * 512], in_=i),
                        inc=smxs, step=16)
                use_xres[sl] = (smxs, tstx)
            psl, ptbf, ptsq, pfc = pend_stats
            emit("tensor", lambda e, l_=psl: e.matmul(PS_A[0:1, :], onesb_sb[:], xbf_st[l_][:],
                                                      start=False, stop=True),
                 wait=[(VE, ptbf)])
            emit("tensor", lambda e, l_=psl: e.matmul(PS_B[0:1, :], onesb_sb[:], sq_st[l_][:],
                                                      start=False, stop=True),
                 wait=[(SE, ptsq)], inc=PE)
            tok_stats = PE.n

            emit("vector", lambda e: e.tensor_scalar(rowt(0, 512), PS_A[0:1, :],
                                                     1.0 / H, None, OP.mult),
                 wait=[(PE, tok_stats)], inc=VE)
            emit("vector", lambda e: e.tensor_scalar(rowt(1, 512), PS_B[0:1, :],
                                                     1.0 / H, None, OP.mult), inc=VE)
            emit("vector", lambda e: e.tensor_mul(rowt(2, 512), rowt(0, 512), rowt(0, 512)), inc=VE)
            emit("vector", lambda e: e.tensor_sub(rowt(3, 512), rowt(1, 512), rowt(2, 512)), inc=VE)
            emit("vector", lambda e: e.tensor_scalar(rowt(1, 512), rowt(3, 512), EPS_LN,
                                                     None, OP.add), inc=VE)
            t8 = emit("vector", lambda e: e.reciprocal(rowt(2, 512), rowt(1, 512)), inc=VE)
            t9 = emit("scalar", lambda e: e.activation(rowt(1, 512), rowt(2, 512), AF.Sqrt),
                      wait=[(VE, t8)], inc=SE)
            t10 = emit("vector", lambda e: e.tensor_mul(rowt(2, 512), rowt(0, 512), rowt(1, 512)),
                       wait=[(SE, t9)], inc=VE)
            wbc = _w(tok_wo_free[0], tok_wo_free[1], (VE, t10))
            rb_ap = PS_SC[:, 1024:1536] if ct == 0 else PS_A[:, :]
            mrb_ap = PS_SC[:, 1536:2048] if ct == 0 else PS_B[:, :]
            emit("tensor", lambda e, a=rb_ap: e.matmul(a, ones1_sb[:], rowt(1, 512),
                                                       start=True, stop=True), wait=wbc)
            emit("tensor", lambda e, a=mrb_ap: e.matmul(a, ones1_sb[:], rowt(2, 512),
                                                        start=True, stop=True), inc=PE)
            tok_bc = PE.n

            for fc in range(NFC):
                par = fc % 3
                sl = fc % 2
                smld = SM("xld%d" % par)
                tld = (smld, emit("sync", lambda e, o=p2x[par], f=fc, c=ct: e.dma_start(
                    out=o, in_=xres_sc[f * 128:(f + 1) * 128, c * 512:(c + 1) * 512]),
                    wait=_w(use_ld[par], use_xres[sl], use_ysc[par], use_tyb[par]),
                    inc=smld, step=16))
                tt1 = emit("vector", lambda e, i=p2x[par], o=p2t[par], a=rb_ap: e.tensor_mul(
                    o, i, a),
                    wait=_w(tld, (PE, tok_bc)), inc=VE)
                ty = emit("vector", lambda e, o=p2x[par], i=p2t[par], a=mrb_ap: e.tensor_sub(
                    o, i, a), wait=_w(use_ysc[par], use_y[par]), inc=VE)
                use_ld[par] = (VE, ty)
                use_xres[sl] = tld
                tyb = emit("scalar", lambda e, i=p2x[par], o=p2yb[par]: e.copy(o, i),
                           wait=_w((VE, ty), use_y[par]), inc=SE)
                use_tyb[par] = (SE, tyb)
                smy, smyb = SM("yst%d" % par), SM("ybst%d" % par)
                tsy = emit("sync", lambda e, i=p2x[par], f=fc, c=ct: e.dma_start(
                    out=y_sc[f * 128:(f + 1) * 128, c * 512:(c + 1) * 512], in_=i),
                    wait=[(VE, ty)], inc=smy, step=16)
                ts2 = emit("sync", lambda e, i=p2yb[par], f=fc, c=ct: e.dma_start(
                    out=ybf_sc[f * 128:(f + 1) * 128, c * 512:(c + 1) * 512], in_=i),
                    wait=[(SE, tyb)], inc=smyb, step=16)
                use_y[par] = (smyb, ts2)
                use_ysc[par] = (smy, tsy)
            if ct == 0:
                tok_c_p2t0 = (VE, VE.n)
                y_done0 = [(SM("ybst%d" % i2), SM("ybst%d" % i2).n) for i2 in range(3)]
        y_done1 = [(SM("ybst%d" % i2), SM("ybst%d" % i2).n) for i2 in range(3)]
        tok_c_pass2 = (VE, VE.n)
        tok_c_pe = (PE, PE.n)

        y_tile = bfv(0, 16384)
        h_buf = bfv(16384, 44032)
        gu = [bfv(60416, 2048), bfv(62464, 2048), bfv(64512, 2048), bfv(66560, 2048)]
        tmp_sg = bfv(68608, 512)
        tmp_us = bfv(69120, 512)
        dslab = [bfv(60416, 5504), bfv(65920, 5504)]
        xbf2 = [bfv(71424, 512), bfv(71936, 512)]
        sq2 = [bfv(72448, 512), bfv(72960, 512)]
        y32b = [f32v(0, 512), f32v(512, 512)]
        xr2 = [f32v(1024, 512), f32v(1536, 512)]
        d2x = [f32v(2048, 512), f32v(2560, 512), f32v(3072, 512)]
        d2t = [f32v(3584, 512), f32v(4096, 512), f32v(4608, 512)]

        tok_ytile_user = None
        use_y32 = [None, None]
        use_xr2 = [None, None]
        use_ds = [None, None]
        prev_q = {0: tok_wo_free[0], 1: tok_wo_free[1], 2: tok_c_p2t0, 3: tok_c_p2t0}
        tok_d_p2 = None
        use_ld2 = [None, None, None]
        use_out = [None, None, None]
        for tt in range(2):
            w0 = _w(*(y_done0 if tt == 0 else y_done1)) + _w(
                tok_ytile_user, (tok_c_pe if tt == 0 else None))
            for g in range(4):
                emit("gpsimd", lambda e, gg=g, t_=tt: e.dma_start(
                    out=y_tile[:, gg * 4096:(gg + 1) * 4096].rearrange("p (kc t) -> p kc t", kc=8),
                    in_=ybf_sc.rearrange("(kc p) t -> p kc t", p=128)[:, gg * 8:(gg + 1) * 8,
                                                                     t_ * 512:(t_ + 1) * 512]),
                    wait=(w0 if g == 0 else ()), inc=SM("ytile"), step=16)
            tok_ytile = (SM("ytile"), SM("ytile").n)

            use_gu = [None, None, None, None]
            tok_guevac = {}
            tok_th_prev = None
            for ic in range(NIC):
                sl = ic % 2
                toks_ld = [None] * 4
                for m in range(2):
                    for half in range(2):
                        bi = m * 2 + half
                        src = d["wg_t"] if m == 0 else d["wu_t"]
                        sm = SM("gu%d" % bi)
                        toks_ld[bi] = (sm, emit("gpsimd", lambda e, s_=gu[bi], a=src, i=ic, hf=half:
                                                e.dma_start(
                                                    out=s_.rearrange("p (kc m_) -> p kc m_", kc=16),
                                                    in_=a[i, :, hf * 16:(hf + 1) * 16, :]),
                                                wait=_w(use_gu[bi])
                                                + (_w(use_ds[0], use_ds[1],
                                                      (tok_bend_pe if tt == 0 else None))
                                                   if ic == 0 else []),
                                                inc=sm, step=16))
                qg = PS_SC[:, (2 * sl) * 512:(2 * sl + 1) * 512]
                qu = PS_SC[:, (2 * sl + 1) * 512:(2 * sl + 2) * 512]
                tok_g = tok_u = None
                for m in range(2):
                    ps = qg if m == 0 else qu
                    tmm = None
                    for half in range(2):
                        bi = m * 2 + half
                        for k in range(16):
                            kc = half * 16 + k
                            tmm = emit("tensor", lambda e, p=ps, s_=gu[bi], kk=k, c=kc,
                                       st=(kc == 0), sp=(kc == NFC - 1):
                                       e.matmul(p, s_[:, kk * 128:(kk + 1) * 128],
                                                y_tile[:, c * 512:(c + 1) * 512],
                                                start=st, stop=sp),
                                       wait=(_w(toks_ld[bi], tok_ytile,
                                                tok_guevac.get((sl, m)) or prev_q[2 * sl + m])
                                             if kc == half * 16 else ()),
                                       inc=(PE if k == 15 else None))
                        use_gu[bi] = (PE, tmm)
                    if m == 0:
                        tok_g = tmm
                    else:
                        tok_u = tmm
                tsg = emit("scalar", lambda e, p=qg, i=ic: e.activation(
                    tmp_sg[:], p, AF.Silu, scale=sg_sb[:, i:i + 1]),
                    wait=_w((PE, tok_g), tok_th_prev), inc=SE)
                tus = emit("vector", lambda e, p=qu, i=ic: e.tensor_scalar_mul(
                    tmp_us[:], p, su_sb[:, i:i + 1]),
                    wait=[(PE, tok_u)], inc=VE)
                if DEBUG_SPILL and ic == 0:
                    tdbg1 = emit("sync", lambda e, t_=tt: e.dma_start(out=dbg_sg[t_], in_=tmp_sg[:]),
                                 wait=[(SE, tsg)], inc=SM("dbg"), step=16)
                    tdbg2 = emit("sync", lambda e, t_=tt: e.dma_start(out=dbg_us[t_], in_=tmp_us[:]),
                                 wait=[(VE, tus)], inc=SM("dbg"), step=16)
                    tok_th_prev = (SM("dbg"), tdbg2)
                th = emit("vector", lambda e, i=ic: e.tensor_mul(
                    h_buf[:, i * 512:(i + 1) * 512], tmp_sg[:], tmp_us[:]),
                    wait=[(SE, tsg), (VE, tus)] + (_w(tok_ytile_user) if ic == 0 else [])
                    + (_w(tok_th_prev) if (DEBUG_SPILL and ic == 0) else []),
                    inc=VE)
                tok_th_prev = (VE, th)
                tok_guevac[(sl, 0)] = (SE, tsg)
                tok_guevac[(sl, 1)] = (VE, tus)
            tok_h_done = (VE, VE.n)
            if DEBUG_SPILL:
                emit("sync", lambda e, t_=tt: e.dma_start(out=dbg_h[t_], in_=h_buf),
                     wait=[tok_h_done], inc=SM("dbg"), step=16)

            tok_dfree = {}
            tok_stats2 = None
            pend_stats2 = None
            tok_stat_pe = [None, None]
            for fc in range(NFC):
                sl = fc % 2
                toks_ld = [None, None]
                for half in range(2):
                    sm = SM("ds%d" % half)
                    toks_ld[half] = (sm, emit("gpsimd", lambda e, s_=dslab[half], f=fc, hf=half:
                                              e.dma_start(
                                                  out=s_.rearrange("p (kc m_) -> p kc m_", kc=43),
                                                  in_=d["wd_t"][f, :, hf * 43:(hf + 1) * 43, :]),
                                              wait=_w(use_ds[half])
                                              + (_w(use_gu[0], use_gu[1], use_gu[2], use_gu[3],
                                                    tok_h_done) if fc == 0 else []),
                                              inc=sm, step=16))
                sm32 = SM("y32%d" % sl)
                ty32 = (sm32, emit("sync", lambda e, o=y32b[sl], f=fc, t_=tt: e.dma_start(
                    out=o, in_=y_sc[f * 128:(f + 1) * 128, t_ * 512:(t_ + 1) * 512]),
                    wait=_w(use_y32[sl])
                    + (_w(tok_c_pass2) if (tt == 0 and fc < 2) else []),
                    inc=sm32, step=16))
                ps = PS_SC[:, sl * 512:(sl + 1) * 512]
                tmm = None
                for half in range(2):
                    for k in range(43):
                        kc = half * 43 + k
                        tmm = emit("tensor", lambda e, p=ps, s_=dslab[half], kk=k, c=kc,
                                   st=(kc == 0), sp=(kc == NIC - 1):
                                   e.matmul(p, s_[:, kk * 128:(kk + 1) * 128],
                                            h_buf[:, c * 512:(c + 1) * 512], start=st, stop=sp),
                                   wait=(_w(toks_ld[half], tok_dfree.get(sl),
                                            tok_h_done) if kc == half * 43 else ()),
                                   inc=(PE if k == 42 else None))
                    use_ds[half] = (PE, tmm)
                if pend_stats2 is not None:
                    psl, ptbf, ptsq, pfc = pend_stats2
                    wst = (_w(tok_c_pass2 if tt == 0 else tok_d_p2)
                           if pfc == 0 else [])
                    emit("tensor", lambda e, l_=psl, st=(pfc == 0): e.matmul(
                        PS_A[0:1, :], onesb_sb[:], xbf2[l_][:], start=st, stop=False),
                        wait=[(VE, ptbf)] + wst)
                    tss = emit("tensor", lambda e, l_=psl, st=(pfc == 0): e.matmul(
                        PS_B[0:1, :], onesb_sb[:], sq2[l_][:], start=st, stop=False),
                        wait=[(SE, ptsq)] + wst, inc=PE)
                    tok_stat_pe[psl] = (PE, tss)
                txr = emit("vector", lambda e, o=xr2[sl], p=ps, f=fc, b_=y32b[sl]:
                           e.scalar_tensor_tensor(o, p, sd_sb[:, f:f + 1], b_, OP.mult, OP.add),
                           wait=_w((PE, tmm), ty32, use_xr2[sl]), inc=VE)
                tok_dfree[sl] = (VE, txr)
                use_y32[sl] = (VE, txr)
                tbf = emit("vector", lambda e, i=xr2[sl], o=xbf2[sl]: e.tensor_copy(o[:], i),
                           wait=_w(tok_stat_pe[sl]), inc=VE)
                tsq = emit("scalar", lambda e, i=xr2[sl], o=sq2[sl]: e.activation(o[:], i, AF.Square),
                           wait=_w((VE, txr), tok_stat_pe[sl]), inc=SE)
                pend_stats2 = (sl, tbf, tsq, fc)
                smxs = SM("xst%d" % sl)
                tsx = emit("sync", lambda e, i=xr2[sl], f=fc, t_=tt: e.dma_start(
                    out=xres_sc[f * 128:(f + 1) * 128, t_ * 512:(t_ + 1) * 512], in_=i),
                    wait=[(VE, txr)], inc=smxs, step=16)
                use_xr2[sl] = (smxs, tsx)
            psl, ptbf, ptsq, pfc = pend_stats2
            emit("tensor", lambda e, l_=psl: e.matmul(PS_A[0:1, :], onesb_sb[:], xbf2[l_][:],
                                                      start=False, stop=True),
                 wait=[(VE, ptbf)])
            emit("tensor", lambda e, l_=psl: e.matmul(PS_B[0:1, :], onesb_sb[:], sq2[l_][:],
                                                      start=False, stop=True),
                 wait=[(SE, ptsq)], inc=PE)
            tok_stats2 = (PE, PE.n)

            emit("vector", lambda e: e.tensor_scalar(rowt(0, 512), PS_A[0:1, :], 1.0 / H,
                                                     None, OP.mult),
                 wait=[(PE, PE.n)], inc=VE)
            emit("vector", lambda e: e.tensor_scalar(rowt(1, 512), PS_B[0:1, :], 1.0 / H,
                                                     None, OP.mult), inc=VE)
            emit("vector", lambda e: e.tensor_mul(rowt(2, 512), rowt(0, 512), rowt(0, 512)), inc=VE)
            emit("vector", lambda e: e.tensor_sub(rowt(3, 512), rowt(1, 512), rowt(2, 512)), inc=VE)
            emit("vector", lambda e: e.tensor_scalar(rowt(1, 512), rowt(3, 512), EPS_LN,
                                                     None, OP.add), inc=VE)
            t8 = emit("vector", lambda e: e.reciprocal(rowt(2, 512), rowt(1, 512)), inc=VE)
            t9 = emit("scalar", lambda e: e.activation(rowt(1, 512), rowt(2, 512), AF.Sqrt),
                      wait=[(VE, t8)], inc=SE)
            t10 = emit("vector", lambda e: e.tensor_mul(rowt(2, 512), rowt(0, 512), rowt(1, 512)),
                       wait=[(SE, t9)], inc=VE)
            wbc = _w(tok_dfree.get(0), tok_dfree.get(1), (VE, t10))
            emit("tensor", lambda e: e.matmul(PS_A[:, :], ones1_sb[:], rowt(1, 512),
                                              start=True, stop=True), wait=wbc)
            emit("tensor", lambda e: e.matmul(PS_B[:, :], ones1_sb[:], rowt(2, 512),
                                              start=True, stop=True), inc=PE)
            tok_bc2 = PE.n

            for fc in range(NFC):
                par = fc % 3
                sl = fc % 2
                smld = SM("xld%d" % par)
                tld = (smld, emit("sync", lambda e, o=d2x[par], f=fc, t_=tt: e.dma_start(
                    out=o, in_=xres_sc[f * 128:(f + 1) * 128, t_ * 512:(t_ + 1) * 512]),
                    wait=_w(use_ld2[par], use_xr2[sl], use_out[par]), inc=smld, step=16))
                tt1 = emit("vector", lambda e, i=d2x[par], o=d2t[par]: e.tensor_mul(
                    o, i, PS_A[:, :]),
                    wait=_w(tld, (PE, tok_bc2)), inc=VE)
                tout = emit("vector", lambda e, o=d2x[par], i=d2t[par]: e.tensor_sub(
                    o, i, PS_B[:, :]), wait=_w(use_out[par]), inc=VE)
                use_ld2[par] = (VE, tout)
                use_xr2[sl] = tld
                smo = SM("ost%d" % par)
                tso = emit("sync", lambda e, i=d2x[par], f=fc, t_=tt: e.dma_start(
                    out=outT[f * 128:(f + 1) * 128, t_ * 512:(t_ + 1) * 512], in_=i),
                    wait=[(VE, tout)], inc=smo, step=16)
                use_out[par] = (smo, tso)
            tok_ytile_user = (PE, PE.n)
            prev_q = {0: tok_dfree.get(0), 1: tok_dfree.get(1),
                      2: tok_guevac.get((1, 0)), 3: tok_guevac.get((1, 1))}
            tok_d_p2 = (VE, VE.n)

        E.ops["sync"].append((None, [(SM("ost%d" % i2), SM("ost%d" % i2).n)
                                     for i2 in range(3)], None))

        @block.tensor
        def _(t):
            E.replay("tensor", t)

        @block.vector
        def _(v):
            E.replay("vector", v)

        @block.scalar
        def _(s):
            E.replay("scalar", s)

        @block.gpsimd
        def _(g):
            E.replay("gpsimd", g)

        @block.sync
        def _(sy):
            E.replay("sync", sy)

    return nc


_PERM_ROPE = [2 * j for j in range(32)] + [2 * j + 1 for j in range(32)]


def _rope_tabs(npos):
    freqs = 1.0 / 10000.0 ** (np.arange(0, DR, 2, dtype=np.float32) / DR)
    t = np.arange(npos, dtype=np.float32)
    ang = np.outer(t, freqs)
    return np.cos(ang).astype(np.float32), np.sin(ang).astype(np.float32)


def _prep_shared(inputs):
    wq_a = np.asarray(inputs["L0_attn_wq_a"], np.float32)
    wq_b = np.asarray(inputs["L0_attn_wq_b"], np.float32)
    wkv_a = np.asarray(inputs["L0_attn_wkv_a"], np.float32)
    wkv_b = np.asarray(inputs["L0_attn_wkv_b"], np.float32)
    wo = np.asarray(inputs["L0_attn_wo"], np.float32)
    w_gate = np.asarray(inputs["L0_ffn_w_gate"], np.int32)
    w_up = np.asarray(inputs["L0_ffn_w_up"], np.int32)
    w_down = np.asarray(inputs["L0_ffn_w_down"], np.int32)

    def tile_w(w, nmc, nkc, mw=128):
        return np.ascontiguousarray(w.reshape(nmc, mw, nkc, 128).transpose(0, 3, 2, 1))

    def to_i8(a):
        return np.ascontiguousarray(a.view(np.int8)[..., 0::4])

    sh = {}
    sh["wqa_t"] = tile_w(wq_a, NKC_QL, NFC)
    rows = []
    for h in range(NH):
        base = h * 192
        rows.extend(range(base, base + 128))
        rows.extend(base + 128 + p for p in _PERM_ROPE)
    wqb_dev = wq_b[np.asarray(rows)]
    sh["wqb_t"] = np.ascontiguousarray(
        wqb_dev.reshape(NH, 192, NKC_QL, 128).transpose(0, 3, 2, 1))
    sh["wkva_c"] = tile_w(wkv_a[:KVL], NCC, NFC)
    wkva_pe = wkv_a[KVL + np.asarray(_PERM_ROPE)]
    sh["wkva_pe"] = np.ascontiguousarray(wkva_pe.reshape(64, NFC, 128).transpose(2, 1, 0))
    kvb = wkv_b.reshape(NH, 256, NCC, 128)
    sh["wkvb_k"] = np.ascontiguousarray(kvb[:, :128].transpose(0, 3, 2, 1))
    sh["wkvb_v"] = np.ascontiguousarray(kvb[:, 128:].transpose(2, 3, 0, 1))
    sh["wo_t"] = tile_w(wo, NFC, NFC)
    sh["wg_t"] = to_i8(tile_w(w_gate, NIC, NFC))
    sh["wu_t"] = to_i8(tile_w(w_up, NIC, NFC))
    sh["wd_t"] = to_i8(tile_w(w_down, NFC, NIC))
    sh["qnw_t"] = np.ascontiguousarray(
        np.asarray(inputs["L0_attn_q_norm_w"], np.float32).reshape(NKC_QL, 128).T)
    sh["kvnw_t"] = np.ascontiguousarray(
        np.asarray(inputs["L0_attn_kv_norm_w"], np.float32).reshape(NCC, 128).T)
    sh["sg_t"] = np.ascontiguousarray(
        np.asarray(inputs["L0_ffn_s_gate"], np.float32).reshape(NIC, 128).T)
    sh["su_t"] = np.ascontiguousarray(
        np.asarray(inputs["L0_ffn_s_up"], np.float32).reshape(NIC, 128).T)
    sh["sd_t"] = np.ascontiguousarray(
        np.asarray(inputs["L0_ffn_s_down"], np.float32).reshape(NFC, 128).T)
    sh["ident"] = np.eye(128, dtype=BF16NP)
    sh["onesb"] = np.ones((128, 1), dtype=BF16NP)
    sh["ones1"] = np.ones((1, 128), dtype=np.float32)
    cosk, sink = _rope_tabs(S)
    sh["coskp"] = np.ascontiguousarray(np.concatenate([cosk.T, cosk.T], axis=0))
    sh["sinkp"] = np.ascontiguousarray(np.concatenate([-sink.T, sink.T], axis=0))
    return sh


def _prep_core(inputs, b, hpar):
    x = np.asarray(inputs["L0_attn_x"], np.float32)[b]
    xT = np.ascontiguousarray(x.T)
    qt = QTILES[hpar]
    own_cols = np.concatenate([np.arange(g * 128, (g + 1) * 128) for g in qt])
    pc = {}
    pc["xT_all"] = xT
    pc["xT_own"] = np.ascontiguousarray(xT[:, own_cols])
    m = np.zeros((128, NSLOT * 2 * 128), np.float32)
    for s in range(NSLOT):
        g = qt[s]
        for j in range(2):
            c = KS[s] - 2 + j
            keys = np.arange(c * 128, (c + 1) * 128)
            qrows = g * 128 + np.arange(128)
            m[:, (s * 2 + j) * 128:(s * 2 + j + 1) * 128] = np.where(
                keys[None, :] <= qrows[:, None], 0.0, MASKVAL)
    pc["mask2"] = m.astype(BF16NP)
    cosk, sink = _rope_tabs(S)
    cq = cosk[own_cols].T
    sq = sink[own_cols].T
    pc["ropeC"] = np.ascontiguousarray(
        (np.concatenate([cq, cq, cq, cq], axis=0) * SCALE).astype(np.float32))
    pc["ropeS"] = np.ascontiguousarray(
        (np.concatenate([-sq, sq, -sq, sq], axis=0) * SCALE).astype(np.float32))
    return pc, own_cols


def kernel(**inputs):
    nc = build_program()
    sh = _prep_shared(inputs)
    in_maps = []
    cols = []
    for core in range(8):
        b, hpar = core // 2, core % 2
        pc, own_cols = _prep_core(inputs, b, hpar)
        m = dict(sh)
        m.update(pc)
        in_maps.append(m)
        cols.append(own_cols)
    res = run_bass_kernel_spmd(nc, in_maps, core_ids=list(range(8)))
    out = np.empty((B, S, H), np.float32)
    for core in range(8):
        b = core // 2
        o = res.results[core]["outT"]
        out[b, cols[core], :] = np.ascontiguousarray(o).T
    return out


# version 07
# speedup vs baseline: 1.0606x; 1.0606x over previous
import math
import numpy as np

DEBUG_SPILL = False
import ml_dtypes
import concourse.bass as bass
import concourse.mybir as mybir
from concourse.bass_utils import run_bass_kernel_spmd

F32 = mybir.dt.float32
BF16 = mybir.dt.bfloat16
I8 = mybir.dt.int8
AF = mybir.ActivationFunctionType
OP = mybir.AluOpType
AX = mybir.AxisListType
BF16NP = ml_dtypes.bfloat16

B, S, H = 4, 2048, 4096
NH, DN, DR, DV = 32, 128, 64, 128
QL, KVL = 1536, 512
INTER = 11008
NIC = INTER // 128
NFC = H // 128
NKC_QL = QL // 128
NCC = KVL // 128
EPS_RMS, EPS_LN = 1e-6, 1e-5
SCALE = 1.0 / math.sqrt(DN + DR)
TQ = 1024
NSLOT = 8
MASKVAL = -1e30

QTILES = [[0, 3, 4, 7, 8, 11, 12, 15], [1, 2, 5, 6, 9, 10, 13, 14]]
KS = [2 * (s + 1) for s in range(NSLOT)]

ENGINES = ("tensor", "vector", "scalar", "gpsimd", "sync")


class Sem:
    def __init__(self, name):
        self.name = name
        self.n = 0
        self.h = None


class Emitter:
    def __init__(self):
        self.ops = {e: [] for e in ENGINES}

    def emit(self, eng, fn, wait=(), inc=None, step=1):
        w = [(s, v) for (s, v) in wait if s is not None and v and v > 0]
        tok = None
        if inc is not None:
            inc.n += step
            tok = inc.n
        self.ops[eng].append((fn, w, (inc, step) if inc is not None else None))
        return tok

    def barrier(self, sems):
        snap = [(s, s.n) for s in sems if s.n > 0]
        for e in ENGINES:
            self.ops[e].append((None, snap, None))

    def replay(self, eng_name, eng):
        waited = {}
        for fn, waits, inc in self.ops[eng_name]:
            for sem, val in waits:
                if waited.get(sem, -1) < val:
                    eng.wait_ge(sem.h, val)
                    waited[sem] = val
            if fn is not None:
                ins = fn(eng)
                if inc is not None:
                    ins.then_inc(inc[0].h, inc[1])


def _w(*tokens):
    return [t for t in tokens if t is not None]


def build_program():
    nc = bass.Bass("TRN2", target_bir_lowering=False)
    E = Emitter()
    PE, VE, SE = Sem("PE"), Sem("VE"), Sem("SE")
    emit = E.emit

    d = {}
    def din(name, shape, dt):
        d[name] = nc.dram_tensor(name, shape, dt, kind="ExternalInput").ap()
    din("xT_all", [H, S], F32)
    din("xT_own", [H, TQ], F32)
    din("wqa_t", [NKC_QL, 128, NFC, 128], F32)
    din("wqb_t", [NH, 128, NKC_QL, 192], F32)
    din("wkva_c", [NCC, 128, NFC, 128], F32)
    din("wkva_pe", [128, NFC, 64], F32)
    din("wkvb_k", [NH, 128, NCC, 128], F32)
    din("wkvb_v", [NCC, 128, NH, 128], F32)
    din("wo_t", [NFC, 128, NFC, 128], F32)
    din("wg_t", [NIC, 128, NFC, 128], I8)
    din("wu_t", [NIC, 128, NFC, 128], I8)
    din("wd_t", [NFC, 128, NIC, 128], I8)
    din("qnw_t", [128, NKC_QL], F32)
    din("kvnw_t", [128, NCC], F32)
    din("sg_t", [128, NIC], F32)
    din("su_t", [128, NIC], F32)
    din("sd_t", [128, NFC], F32)
    din("ident", [128, 128], BF16)
    din("onesb", [128, 1], BF16)
    din("ones1", [1, 128], F32)
    din("mask2", [128, NSLOT * 2 * 128], BF16)
    din("ropeC", [128, TQ], F32)
    din("ropeS", [128, TQ], F32)
    din("coskp", [64, S], F32)
    din("sinkp", [64, S], F32)

    sck = "ExternalOutput" if DEBUG_SPILL else "Internal"
    outT = nc.dram_tensor("outT", [H, TQ], F32, kind="ExternalOutput").ap()
    attn_sc = nc.dram_tensor("attn_sc", [H, TQ], BF16, kind=sck).ap()
    xres_sc = nc.dram_tensor("xres_sc", [H, TQ], F32, kind=sck).ap()
    y_sc = nc.dram_tensor("y_sc", [H, TQ], F32, kind=sck).ap()
    ybf_sc = nc.dram_tensor("ybf_sc", [H, TQ], BF16, kind=sck).ap()
    if DEBUG_SPILL:
        dbg_qn = nc.dram_tensor("dbg_qn", [128, NKC_QL * TQ], BF16, kind="ExternalOutput").ap()
        dbg_kvn = nc.dram_tensor("dbg_kvn", [128, NCC * S], BF16, kind="ExternalOutput").ap()
        dbg_kpe = nc.dram_tensor("dbg_kpe", [64, S], BF16, kind="ExternalOutput").ap()
        dbg_h = nc.dram_tensor("dbg_h", [2, 128, NIC * 512], BF16, kind="ExternalOutput").ap()
        dbg_x1 = nc.dram_tensor("dbg_x1", [H, TQ], F32, kind="ExternalOutput").ap()
        dbg_sg = nc.dram_tensor("dbg_sg", [2, 128, 512], BF16, kind="ExternalOutput").ap()
        dbg_us = nc.dram_tensor("dbg_us", [2, 128, 512], BF16, kind="ExternalOutput").ap()

    import contextlib
    with contextlib.ExitStack() as stack:
        ec = stack.enter_context
        ident_sb = ec(nc.sbuf_tensor("ident_sb", [128, 128], BF16))
        onesb_sb = ec(nc.sbuf_tensor("onesb_sb", [128, 1], BF16))
        ones1_sb = ec(nc.sbuf_tensor("ones1_sb", [1, 128], F32))
        qnw_sb = ec(nc.sbuf_tensor("qnw_sb", [128, NKC_QL], F32))
        kvnw_sb = ec(nc.sbuf_tensor("kvnw_sb", [128, NCC], F32))
        sg_sb = ec(nc.sbuf_tensor("sg_sb", [128, NIC], F32))
        su_sb = ec(nc.sbuf_tensor("su_sb", [128, NIC], F32))
        sd_sb = ec(nc.sbuf_tensor("sd_sb", [128, NFC], F32))
        rows = ec(nc.sbuf_tensor("rows", [1, 4096], F32))
        zbuf = ec(nc.sbuf_tensor("zbuf", [128, 64], F32))
        pbf = ec(nc.sbuf_tensor("pbf", [128, 73472], BF16))
        pf32 = ec(nc.sbuf_tensor("pf32", [128, 7168], F32))
        PS_SC = ec(nc.psum_tensor("PS_SC", [128, 2048], F32))
        PS_PTA = ec(nc.psum_tensor("PS_PTA", [128, 1024], BF16))
        PS_PTB = ec(nc.psum_tensor("PS_PTB", [128, 1024], BF16))
        PS_A = ec(nc.psum_tensor("PS_A", [128, 512], F32))
        PS_B = ec(nc.psum_tensor("PS_B", [128, 512], F32))
        PE.h = ec(nc.semaphore("sPE"))
        VE.h = ec(nc.semaphore("sVE"))
        SE.h = ec(nc.semaphore("sSE"))
        block = ec(nc.Block())

        sems = {}
        def SM(name):
            if name not in sems:
                s_ = Sem(name)
                s_.h = ec(nc.semaphore("sm_" + name))
                sems[name] = s_
            return sems[name]
        def allsems():
            return [PE, VE, SE] + list(sems.values())

        def bfv(col, n):
            return pbf[:, col:col + n]
        def f32v(col, n):
            return pf32[:, col:col + n]
        def rowt(r, n):
            return rows[0:1, r * 1024:r * 1024 + n]

        qn = bfv(45568, NKC_QL * TQ)
        kvn = bfv(57856, NCC * S)
        kpe = bfv(66048, S)[0:64, :]
        mask2_sb = bfv(68096, NSLOT * 2 * 128)

        for (dst, src) in [
            (ident_sb[:], d["ident"]), (onesb_sb[:], d["onesb"]), (ones1_sb[:], d["ones1"]),
            (qnw_sb[:], d["qnw_t"]), (kvnw_sb[:], d["kvnw_t"]),
            (sg_sb[:], d["sg_t"]), (su_sb[:], d["su_t"]), (sd_sb[:], d["sd_t"]),
            (mask2_sb[:], d["mask2"]),
        ]:
            emit("sync", lambda e, a=dst, b=src: e.dma_start(out=a, in_=b), inc=SM("consts"), step=16)
        tok_consts = (SM("consts"), SM("consts").n)

        xab = [bfv(0, 16384), bfv(25088, 16384)]
        aslab = [bfv(16384, 4096), bfv(20480, 4096), bfv(41472, 4096)]
        sqst = bfv(24576, 512)
        cosk = pf32[0:64, 2048:4096]
        sink = pf32[0:64, 4096:6144]
        krA = pf32[:, 6144:6656]
        krB = pf32[:, 6656:7168]

        emit("sync", lambda e: e.dma_start(out=cosk, in_=d["coskp"]), inc=SM("consts"), step=16)
        emit("sync", lambda e: e.dma_start(out=sink, in_=d["sinkp"]), inc=SM("consts"), step=16)
        tok_consts = (SM("consts"), SM("consts").n)

        use_xa = [None, None]
        use_slab = [None, None, None]
        use_sq = None
        ps_free = {}

        slab_jobs = []
        for _q in range(4):
            for _mc in range(NCC):
                slab_jobs.append((d["wkva_c"][_mc], 128))
            slab_jobs.append((d["wkva_pe"], 64))
        for _q in range(2):
            for _mc in range(NKC_QL):
                slab_jobs.append((d["wqa_t"][_mc], 128))
        slab_tok = [None] * len(slab_jobs)
        slab_emitted = [0]

        def slab_emit_next():
            j = slab_emitted[0]
            if j >= len(slab_jobs):
                return
            b = j % 3
            srcj, mw = slab_jobs[j]
            sm = SM("aslab%d" % b)
            slab_tok[j] = (sm, emit("gpsimd", lambda e, s_=aslab[b], a=srcj, m_=mw: e.dma_start(
                out=s_.rearrange("p (kc m) -> p kc m", kc=NFC)[:, :, 0:m_], in_=a),
                wait=_w(use_slab[b]), inc=sm, step=16))
            slab_emitted[0] += 1

        def proj_quarter(xa, tok_xa, nmc, dst_cols_fn, nw_sb, nfeat, pe_chunk, jbase, q4):
            nonlocal use_sq
            for mc in range(nmc + (1 if pe_chunk else 0)):
                is_pe = pe_chunk and mc == nmc
                j = jbase + mc
                sl = j % 3
                slab = aslab[sl]
                mwid = 64 if is_pe else 128
                while slab_emitted[0] <= j:
                    slab_emit_next()
                tok_slab = slab_tok[j]
                if is_pe:
                    ps = PS_SC[0:64, 1024:1536]
                    pskey = 2
                else:
                    ps = PS_SC[:, sl * 512:(sl + 1) * 512]
                    pskey = sl
                wmm = _w(ps_free.get(pskey), tok_xa, tok_slab)
                tmm = None
                for fc in range(NFC):
                    tmm = emit("tensor", lambda e, p=ps, s_=slab, f=fc, mw=mwid,
                               st=(fc == 0), sp=(fc == NFC - 1):
                               e.matmul(p, s_[:, f * 128:f * 128 + mw],
                                        xa[:, f * 512:(f + 1) * 512], start=st, stop=sp),
                               wait=(wmm if fc == 0 else ()),
                               inc=(PE if fc == NFC - 1 else None))
                use_slab[sl] = (PE, tmm)
                slab_emit_next()
                if not is_pe:
                    st_f32 = f32v(mc * 512, 512)
                    tok_ev = emit("vector", lambda e, o=st_f32, p=ps: e.tensor_copy(o, p),
                                  wait=[(PE, tmm)], inc=VE)
                    ps_free[pskey] = (VE, tok_ev)
                    tok_sq = emit("scalar", lambda e, i=st_f32: e.activation(sqst[:], i, AF.Square),
                                  wait=_w((VE, tok_ev), use_sq), inc=SE)
                    tok_ss = emit("tensor", lambda e, st=(mc == 0), sp=(mc == nmc - 1):
                                  e.matmul(PS_A[0:1, :], onesb_sb[:], sqst[:], start=st, stop=sp),
                                  wait=[(SE, tok_sq)] + (_w(ps_free.get("ss")) if mc == 0 else []),
                                  inc=PE)
                    use_sq = (PE, tok_ss)
                else:
                    tA = emit("vector", lambda e: e.tensor_copy(krA[0:64, :], PS_SC[0:64, 1024:1536]),
                              wait=[(PE, tmm)], inc=VE)
                    ps_free[pskey] = (VE, tA)
                    tb1 = emit("scalar", lambda e: e.copy(krB[0:32, :], krA[32:64, :]),
                               wait=[(VE, tA)], inc=SE)
                    tb2 = emit("scalar", lambda e: e.copy(krB[32:64, :], krA[0:32, :]),
                               wait=[(VE, tA)], inc=SE)
                    ck = cosk[:, q4 * 512:(q4 + 1) * 512]
                    sk = sink[:, q4 * 512:(q4 + 1) * 512]
                    emit("vector", lambda e, c=ck: e.tensor_mul(krA[64:128, :], krA[0:64, :], c),
                         wait=_w((SE, tb2), tok_consts), inc=VE)
                    emit("vector", lambda e, s_=sk: e.tensor_mul(krB[64:128, :], krB[0:64, :], s_),
                         inc=VE)
                    emit("vector", lambda e, c=q4: e.tensor_add(
                        kpe[:, c * 512:(c + 1) * 512], krA[64:128, :], krB[64:128, :]), inc=VE)
            tr1 = emit("vector", lambda e, nf=nfeat: e.tensor_scalar(
                rowt(0, 512), PS_A[0:1, :], 1.0 / nf, EPS_RMS, OP.mult, OP.add),
                wait=[(PE, PE.n)], inc=VE)
            ps_free["ss"] = (VE, tr1)
            emit("vector", lambda e: e.reciprocal(rowt(1, 512), rowt(0, 512)), inc=VE)
            tr3 = emit("scalar", lambda e: e.activation(rowt(2, 512), rowt(1, 512), AF.Sqrt),
                       wait=[(VE, VE.n)], inc=SE)
            tbc = emit("tensor", lambda e: e.matmul(PS_SC[:, 1536:2048], ones1_sb[:], rowt(2, 512),
                                                    start=True, stop=True),
                       wait=[(SE, tr3)] + _w(ps_free.get(3)), inc=PE)
            tkn = None
            for mc in range(nmc):
                tkn = emit("vector", lambda e, m=mc: e.scalar_tensor_tensor(
                    dst_cols_fn(m), f32v(m * 512, 512), nw_sb[:, m:m + 1], PS_SC[:, 1536:2048],
                    OP.mult, OP.mult),
                    wait=[(PE, tbc)], inc=VE)
            ps_free[3] = (VE, tkn)

        def a_src(qi):
            if qi < 4:
                return d["xT_all"].rearrange("(fc p) t -> p fc t", p=128)[
                    :, :, qi * 512:(qi + 1) * 512]
            oq = qi - 4
            return d["xT_own"].rearrange("(fc p) t -> p fc t", p=128)[
                :, :, oq * 512:(oq + 1) * 512]

        tok_xa_ld = [None] * 6
        def a_load(qi):
            sm = SM("xa%d" % (qi % 2))
            t_ = emit("gpsimd", lambda e, a=a_src(qi), x=xab[qi % 2]: e.dma_start(
                out=x[:].rearrange("p (fc t) -> p fc t", fc=NFC), in_=a),
                wait=_w(use_xa[qi % 2]), inc=sm, step=16)
            tok_xa_ld[qi] = (sm, t_)

        a_load(0)
        a_load(1)
        slab_emit_next()
        slab_emit_next()
        for qi in range(6):
            if qi < 4:
                proj_quarter(xab[qi % 2], tok_xa_ld[qi], NCC,
                             (lambda m, c=qi: kvn[:, m * S + c * 512:m * S + c * 512 + 512]),
                             kvnw_sb, KVL, True, qi * (NCC + 1), qi)
            else:
                oq = qi - 4
                proj_quarter(xab[qi % 2], tok_xa_ld[qi], NKC_QL,
                             (lambda m, c=oq: qn[:, m * TQ + c * 512:m * TQ + c * 512 + 512]),
                             qnw_sb, QL, False, 4 * (NCC + 1) + oq * NKC_QL, 0)
            use_xa[qi % 2] = (PE, PE.n)
            if qi + 2 < 6:
                a_load(qi + 2)

        if DEBUG_SPILL:
            emit("sync", lambda e: e.dma_start(out=dbg_qn, in_=qn), wait=[(VE, VE.n)],
                 inc=SM("dbg"), step=16)
            emit("sync", lambda e: e.dma_start(out=dbg_kvn, in_=kvn), inc=SM("dbg"), step=16)
            emit("sync", lambda e: e.dma_start(out=dbg_kpe, in_=kpe), inc=SM("dbg"), step=16)

        E.barrier(allsems())

        kslab = [bfv(0, 2048), bfv(2048, 2048)]
        vw = [bfv(4096, 2048), bfv(6144, 2048)]
        qbs = [bfv(8192, 2304), bfv(10496, 2304)]
        knope = bfv(12800, 4 * 2048)
        vbuf = bfv(20992, 16 * 512)
        qnope = [bfv(29184, 1024), bfv(30208, 1024)]
        qpe = [bfv(31232, 1024)[0:64, :], bfv(32256, 1024)[0:64, :]]
        Pt = bfv(33280, 2048)
        Pn = bfv(35328, 2048)
        PT = bfv(37376, 2048)
        attn_h = [bfv(39424, 1024), bfv(40448, 1024)]
        ropeA = f32v(0, 1024)
        ropeB = f32v(1024, 1024)
        ropeT1 = f32v(2048, 1024)
        ropeT2 = f32v(3072, 1024)
        ropeC_v = f32v(4096, 1024)
        ropeS_v = f32v(5120, 1024)

        emit("sync", lambda e: e.dma_start(out=ropeC_v, in_=d["ropeC"]), inc=SM("consts"), step=16)
        emit("sync", lambda e: e.dma_start(out=ropeS_v, in_=d["ropeS"]), inc=SM("consts"), step=16)
        tok_bconsts = (SM("consts"), SM("consts").n)

        use_group = [None, None]
        use_qbs = [None, None]
        use_attn = [None, None]
        pend = {}
        tok_tr_pe_prev = None

        def flush_A():
            if not pend or pend.get("pn") is not None:
                return
            z, n = pend["zcol"], pend["ncols"]
            trz = emit("vector", lambda e, z_=z: e.reciprocal(
                zbuf[:, 16 + z_:17 + z_], zbuf[:, z_:z_ + 1]),
                wait=[(SE, pend["exp"])], inc=VE)
            pend["pn"] = emit("vector", lambda e, n_=n, z_=z: e.tensor_scalar_mul(
                Pn[:, 0:n_], Pt[:, 0:n_], zbuf[:, 16 + z_:17 + z_]),
                wait=_w((VE, trz), tok_tr_pe_prev), inc=VE)

        def flush_B():
            nonlocal tok_tr_pe_prev
            if not pend:
                return
            flush_A()
            s, Ks, h4, i = pend["s"], pend["Ks"], pend["h4"], pend["i"]
            nwave = (Ks + 7) // 8
            tok_tr_evac = None
            for wv_ in range(nwave):
                ptps = PS_PTA if wv_ % 2 == 0 else PS_PTB
                wlen = min(8, Ks - wv_ * 8)
                ttr = None
                for j in range(wlen):
                    tc = wv_ * 8 + j
                    ttr = emit("tensor", lambda e, p=ptps, jj=j, t=tc:
                               e.transpose(p[:, jj * 128:(jj + 1) * 128],
                                           Pn[:, t * 128:(t + 1) * 128], ident_sb[:]),
                               wait=(_w((VE, pend["pn"]), tok_pt_evac[wv_ % 2])
                                     if j == 0 else ()),
                               inc=(PE if j == wlen - 1 else None))
                tev = emit("vector", lambda e, p=ptps, w_=wv_, l=wlen: e.tensor_copy(
                    PT[:, w_ * 1024:w_ * 1024 + l * 128], p[:, 0:l * 128]),
                    wait=[(PE, ttr)], inc=VE)
                tok_pt_evac[wv_ % 2] = (VE, tev)
                tok_tr_evac = (VE, tev)
            tok_tr_pe_prev = (PE, PE.n)
            psv_i = s % 2
            psv = PS_A if psv_i == 0 else PS_B
            tmm = None
            for tc in range(Ks):
                tmm = emit("tensor", lambda e, p=psv, t=tc, h4_=h4,
                           st=(tc == 0), sp=(tc == Ks - 1):
                           e.matmul(p[:, 0:128],
                                    vbuf[:, t * 512 + h4_ * 128:t * 512 + (h4_ + 1) * 128],
                                    PT[:, t * 128:(t + 1) * 128], start=st, stop=sp),
                           wait=(_w(tok_tr_evac, tok_av_evac[psv_i], tok_v_ready)
                                 if tc == 0 else ()),
                           inc=(PE if tc == Ks - 1 else None))
            tev = emit("scalar", lambda e, p=psv, ss=s, a=pend["attn"]: e.copy(
                a[:, ss * 128:(ss + 1) * 128], p[:, 0:128]),
                wait=[(PE, tmm)] + (_w(use_attn[i]) if s == 0 else []),
                inc=SE)
            tok_av_evac[psv_i] = (SE, tev)
            if s == NSLOT - 1:
                sm = SM("ast%d" % i)
                tst = emit("sync", lambda e, a=pend["attn"], h=pend["hh"]: e.dma_start(
                    out=attn_sc[h * 128:(h + 1) * 128, :], in_=a),
                    wait=[(SE, SE.n)], inc=sm, step=16)
                use_attn[i] = (sm, tst)
            pend.clear()
        tok_av_evac = {0: None, 1: None}
        tok_pt_evac = {0: None, 1: None}
        tok_knope_ready = None
        tok_v_ready = None
        tok_scores_free = None
        tok_qn_evac = [None, None]
        tok_rope_done = None

        for pair in range(16):
            he, ho = 2 * pair, 2 * pair + 1
            if pair % 2 == 0:
                flush_B()
                g4 = pair // 2
                sl = g4 % 2
                smk, smv = SM("kv%d" % sl), SM("vw%d" % sl)
                tok_k = (smk, emit("gpsimd", lambda e, s_=kslab[sl], g=g4: e.dma_start(
                    out=s_.rearrange("p (h cc d) -> p h cc d", h=4, cc=NCC),
                    in_=d["wkvb_k"][4 * g:4 * g + 4].rearrange("h p cc d -> p h cc d")),
                    wait=_w(use_group[sl]), inc=smk, step=16))
                tok_vw = (smv, emit("gpsimd", lambda e, s_=vw[sl], g=g4: e.dma_start(
                    out=s_.rearrange("p (cc hd) -> p cc hd", cc=NCC),
                    in_=d["wkvb_v"][:, :, 4 * g:4 * g + 4, :].rearrange("cc p h d -> p cc (h d)")),
                    wait=_w(use_group[sl]), inc=smv, step=16))
                wv = _w(use_group[0], use_group[1])
                for tc in range(16):
                    psv = PS_A if tc % 2 == 0 else PS_B
                    wk = _w(tok_vw, tok_av_evac[tc % 2]) + (wv if tc < 2 else [])
                    tmm = None
                    for cc in range(NCC):
                        tmm = emit("tensor", lambda e, p=psv, c=cc, t=tc, s_=vw[sl],
                                   st=(cc == 0), sp=(cc == NCC - 1):
                                   e.matmul(p[:, 0:512], kvn[:, c * S + t * 128:c * S + t * 128 + 128],
                                            s_[:, c * 512:(c + 1) * 512], start=st, stop=sp),
                                   wait=(wk if cc == 0 else ()),
                                   inc=(PE if cc == NCC - 1 else None))
                    tev = emit("vector", lambda e, p=psv, t=tc: e.tensor_copy(
                        vbuf[:, t * 512:(t + 1) * 512], p[:, 0:512]),
                        wait=[(PE, tmm)], inc=VE)
                    tok_av_evac[tc % 2] = (VE, tev)
                tok_v_ready = (VE, VE.n)
                for hh4 in range(4):
                    for nb in range(4):
                        psv = PS_A if nb % 2 == 0 else PS_B
                        wk = _w(tok_k, tok_av_evac[nb % 2])
                        tmm = None
                        for cc in range(NCC):
                            tmm = emit("tensor", lambda e, p=psv, h=hh4, c=cc, n=nb, s_=kslab[sl],
                                       st=(cc == 0), sp=(cc == NCC - 1):
                                       e.matmul(p[:, 0:512],
                                                s_[:, (h * NCC + c) * 128:(h * NCC + c) * 128 + 128],
                                                kvn[:, c * S + n * 512:c * S + n * 512 + 512],
                                                start=st, stop=sp),
                                       wait=(wk if cc == 0 else ()),
                                       inc=(PE if cc == NCC - 1 else None))
                        tev = emit("vector", lambda e, p=psv, h=hh4, n=nb: e.tensor_copy(
                            knope[:, h * 2048 + n * 512:h * 2048 + n * 512 + 512], p[:, 0:512]),
                            wait=[(PE, tmm)], inc=VE)
                        tok_av_evac[nb % 2] = (VE, tev)
                tok_knope_ready = (VE, VE.n)

            tok_qbs_ld = [None, None]
            for i, hh in enumerate((he, ho)):
                sm = SM("qbs%d" % i)
                tok_qbs_ld[i] = (sm, emit("gpsimd", lambda e, s_=qbs[i], h=hh: e.dma_start(
                    out=s_.rearrange("p (kc m) -> p kc m", kc=NKC_QL), in_=d["wqb_t"][h]),
                    wait=_w(use_qbs[i]), inc=sm, step=16))

            for i, hh in enumerate((he, ho)):
                wq = _w(tok_qbs_ld[i], tok_scores_free,
                        tok_qn_evac[0], tok_qn_evac[1])
                tmm = None
                for nb in range(2):
                    for kc in range(NKC_QL):
                        tmm = emit("tensor", lambda e, n=nb, k=kc, s_=qbs[i],
                                   st=(kc == 0), sp=(kc == NKC_QL - 1):
                                   e.matmul(PS_SC[:, n * 512:(n + 1) * 512],
                                            s_[:, k * 192:k * 192 + 128],
                                            qn[:, k * TQ + n * 512:k * TQ + n * 512 + 512],
                                            start=st, stop=sp),
                                   wait=(wq if (nb == 0 and kc == 0) else ()),
                                   inc=(PE if (nb == 1 and kc == NKC_QL - 1) else None))
                tev = emit("scalar", lambda e, o=qnope[i]: e.activation(
                    o, PS_SC[:, 0:1024], AF.Copy, scale=SCALE),
                    wait=[(PE, tmm)], inc=SE)
                tok_qn_evac[i] = (SE, tev)
                wr = _w(tok_rope_done, tok_scores_free) if i == 0 else [(VE, VE.n)]
                tmm = None
                for nb in range(2):
                    for kc in range(NKC_QL):
                        tmm = emit("tensor", lambda e, n=nb, k=kc, s_=qbs[i],
                                   st=(kc == 0), sp=(kc == NKC_QL - 1):
                                   e.matmul(PS_SC[0:64, 1024 + n * 512:1024 + (n + 1) * 512],
                                            s_[:, k * 192 + 128:k * 192 + 192],
                                            qn[:, k * TQ + n * 512:k * TQ + n * 512 + 512],
                                            start=st, stop=sp),
                                   wait=(wr if (nb == 0 and kc == 0) else ()),
                                   inc=(PE if (nb == 1 and kc == NKC_QL - 1) else None))
                use_qbs[i] = (PE, tmm)
                base = 0 if i == 0 else 64
                trv = emit("vector", lambda e, b_=base: e.tensor_copy(
                    ropeA[b_:b_ + 64, :], PS_SC[0:64, 1024:2048]),
                    wait=[(PE, tmm)], inc=VE)
                tok_rope_done = (VE, trv)
            tb = None
            for (db, sb_) in [(0, 32), (32, 0), (64, 96), (96, 64)]:
                tb = emit("scalar", lambda e, a=db, b_=sb_: e.copy(
                    ropeB[a:a + 32, :], ropeA[b_:b_ + 32, :]),
                    wait=[(VE, tok_rope_done[1])], inc=SE)
            emit("vector", lambda e: e.tensor_mul(ropeT1[:], ropeA[:], ropeC_v[:]),
                 wait=_w((SE, tb), tok_bconsts), inc=VE)
            emit("vector", lambda e: e.tensor_mul(ropeT2[:], ropeB[:], ropeS_v[:]), inc=VE)
            emit("vector", lambda e: e.tensor_add(qpe[0][:, :], ropeT1[0:64, :], ropeT2[0:64, :]),
                 inc=VE)
            tq2 = emit("vector", lambda e: e.tensor_add(
                qpe[1][:, :], ropeT1[64:128, :], ropeT2[64:128, :]), inc=VE)
            tok_qpe = (VE, tq2)

            for i, hh in enumerate((he, ho)):
                h4 = hh % 4
                for s in range(NSLOT):
                    Ks = KS[s]
                    ncols = Ks * 128
                    nbk = (ncols + 511) // 512
                    wsc = _w(tok_scores_free, tok_qn_evac[i], tok_qpe, tok_knope_ready,
                             tok_consts)
                    first = True
                    for nb in range(nbk):
                        c0 = nb * 512
                        c1 = min(ncols, c0 + 512)
                        emit("tensor", lambda e, a=c0, b_=c1, q=qnope[i], h4_=h4, ss=s:
                             e.matmul(PS_SC[:, a:b_], q[:, ss * 128:(ss + 1) * 128],
                                      knope[:, h4_ * 2048 + a:h4_ * 2048 + b_],
                                      start=True, stop=False),
                             wait=(wsc if first else ()))
                        first = False
                        emit("tensor", lambda e, a=c0, b_=c1, qp=qpe[i], ss=s,
                             sp=(c1 < ncols):
                             e.matmul(PS_SC[:, a:b_], qp[:, ss * 128:(ss + 1) * 128],
                                      kpe[:, a:b_], start=False, stop=sp))
                    tok_sc = None
                    for j in range(2):
                        c0 = (Ks - 2 + j) * 128
                        tok_sc = emit("tensor", lambda e, a=c0, ss=s, jj=j, sp=(j == 1):
                                      e.matmul(PS_SC[:, a:a + 128], ident_sb[:],
                                               mask2_sb[:, (ss * 2 + jj) * 128:(ss * 2 + jj + 1) * 128],
                                               start=False, stop=sp),
                                      inc=(PE if j == 1 else None))
                    flush_A()
                    zcol = i * 8 + s
                    tmax = emit("vector", lambda e, n=ncols, z=zcol: e.reduce_max(
                        zbuf[:, 32 + z:33 + z], PS_SC[:, 0:n], AX.X, negate=True),
                        wait=_w((PE, tok_sc), tok_scores_free), inc=VE)
                    texp = emit("scalar", lambda e, n=ncols, z=zcol: e.activation(
                        Pt[:, 0:n], PS_SC[:, 0:n], AF.Exp, bias=zbuf[:, 32 + z:33 + z],
                        accum_out=zbuf[:, z:z + 1]),
                        wait=[(VE, tmax)], inc=SE)
                    tok_scores_free = (SE, texp)
                    flush_B()
                    pend.update(i=i, hh=hh, h4=h4, s=s, Ks=Ks, ncols=ncols, zcol=zcol,
                                exp=texp, pn=None, attn=attn_h[i])
            use_group[(ho // 4) % 2] = (PE, PE.n)
        flush_B()
        tok_bend_pe = (PE, PE.n)
        tok_bend_ve = (VE, VE.n)
        tok_bend_se = (SE, SE.n)
        tok_ast = [(SM("ast0"), SM("ast0").n), (SM("ast1"), SM("ast1").n)]

        rhsb = [bfv(0, 16384), bfv(16384, 16384)]
        wos = [bfv(32768, 4096), bfv(36864, 4096)]
        xbf_st = [bfv(40960, 512), bfv(41472, 512)]
        sq_st = [bfv(41984, 512), bfv(42496, 512)]
        p2yb = [bfv(43008, 512), bfv(43520, 512), bfv(44544, 512)]
        x32b = [f32v(0, 512), f32v(512, 512)]
        xres_st = [f32v(1024, 512), f32v(1536, 512)]
        p2x = [f32v(2048, 512), f32v(2560, 512), f32v(3072, 512)]
        p2t = [f32v(3584, 512), f32v(4096, 512), f32v(4608, 512)]

        use_wos = [None, None]
        use_x32 = [None, None]
        use_xres = [None, None]
        use_ld = [None, None, None]
        use_y = [None, None, None]
        use_ysc = [None, None, None]
        use_tyb = [None, None, None]
        tok_wo_free = [None, None]
        tok_stat_pe = [None, None]
        tok_rhs_t = [None, None]
        for c_ in range(2):
            smr = SM("rhs%d" % c_)
            for g in range(2):
                emit("sync", lambda e, gg=g, c=c_: e.dma_start(
                    out=rhsb[c][:, gg * 8192:(gg + 1) * 8192].rearrange("p (kc t) -> p kc t", kc=16),
                    in_=attn_sc.rearrange("(kc p) t -> p kc t", p=128)[
                        :, gg * 16:(gg + 1) * 16, c * 512:(c + 1) * 512]),
                    wait=(_w(tok_bend_pe, tok_bend_ve, *tok_ast) if g == 0 else ()),
                    inc=smr, step=16)
            tok_rhs_t[c_] = (smr, smr.n)
        for ct in range(2):
            rhs = rhsb[ct]
            tok_rhs = tok_rhs_t[ct]

            pend_stats = None
            for fc in range(NFC):
                sl = fc % 2
                smw, smx = SM("wo%d" % sl), SM("x32%d" % sl)
                first2 = (ct == 0 and fc < 2)
                tok_ws = (smw, emit("gpsimd", lambda e, s_=wos[sl], f=fc: e.dma_start(
                    out=s_.rearrange("p (kc m) -> p kc m", kc=NFC), in_=d["wo_t"][f]),
                    wait=_w(use_wos[sl]) + (_w(tok_bend_pe, tok_bend_ve, *tok_ast)
                                            if first2 else []),
                    inc=smw, step=16))
                tok_x32 = (smx, emit("sync", lambda e, b_=x32b[sl], f=fc, c=ct: e.dma_start(
                    out=b_, in_=d["xT_own"][f * 128:(f + 1) * 128, c * 512:(c + 1) * 512]),
                    wait=_w(use_x32[sl]) + (_w(tok_bend_ve, tok_bend_se) if first2 else []),
                    inc=smx, step=16))
                ps = PS_SC[:, sl * 512:(sl + 1) * 512]
                wmm = _w(tok_ws, tok_rhs, tok_wo_free[sl]) + (
                    _w(tok_scores_free) if first2 else [])
                tmm = None
                for kc in range(NFC):
                    tmm = emit("tensor", lambda e, p=ps, k=kc, s_=wos[sl], r_=rhs,
                               st=(kc == 0), sp=(kc == NFC - 1):
                               e.matmul(p, s_[:, k * 128:(k + 1) * 128],
                                        r_[:, k * 512:(k + 1) * 512], start=st, stop=sp),
                               wait=(wmm if kc == 0 else ()),
                               inc=(PE if kc == NFC - 1 else None))
                use_wos[sl] = (PE, tmm)
                if pend_stats is not None:
                    psl, ptbf, ptsq, pfc = pend_stats
                    wst = _w(tok_bend_se) if (ct == 0 and pfc == 0) else []
                    emit("tensor", lambda e, l_=psl, st=(pfc == 0): e.matmul(
                        PS_A[0:1, :], onesb_sb[:], xbf_st[l_][:], start=st, stop=False),
                        wait=[(VE, ptbf)] + wst)
                    tss = emit("tensor", lambda e, l_=psl, st=(pfc == 0): e.matmul(
                        PS_B[0:1, :], onesb_sb[:], sq_st[l_][:], start=st, stop=False),
                        wait=[(SE, ptsq)] + wst, inc=PE)
                    tok_stat_pe[psl] = (PE, tss)
                tres = emit("vector", lambda e, o=xres_st[sl], p=ps, b_=x32b[sl]: e.tensor_add(
                    o, p, b_),
                    wait=_w((PE, tmm), tok_x32, use_xres[sl]), inc=VE)
                tok_wo_free[sl] = (VE, tres)
                use_x32[sl] = (VE, tres)
                tbf = emit("vector", lambda e, i=xres_st[sl], o=xbf_st[sl]: e.tensor_copy(o[:], i),
                           wait=_w(tok_stat_pe[sl]) + (_w(*tok_ast) if first2 else []),
                           inc=VE)
                tsq = emit("scalar", lambda e, i=xres_st[sl], o=sq_st[sl]: e.activation(
                    o[:], i, AF.Square),
                    wait=_w((VE, tres), tok_stat_pe[sl]), inc=SE)
                pend_stats = (sl, tbf, tsq, fc)
                smxs = SM("xst%d" % sl)
                tstx = emit("sync", lambda e, i=xres_st[sl], f=fc, c=ct: e.dma_start(
                    out=xres_sc[f * 128:(f + 1) * 128, c * 512:(c + 1) * 512], in_=i),
                    wait=[(VE, tres)], inc=smxs, step=16)
                if DEBUG_SPILL:
                    tstx = emit("sync", lambda e, i=xres_st[sl], f=fc, c=ct: e.dma_start(
                        out=dbg_x1[f * 128:(f + 1) * 128, c * 512:(c + 1) * 512], in_=i),
                        inc=smxs, step=16)
                use_xres[sl] = (smxs, tstx)
            psl, ptbf, ptsq, pfc = pend_stats
            emit("tensor", lambda e, l_=psl: e.matmul(PS_A[0:1, :], onesb_sb[:], xbf_st[l_][:],
                                                      start=False, stop=True),
                 wait=[(VE, ptbf)])
            emit("tensor", lambda e, l_=psl: e.matmul(PS_B[0:1, :], onesb_sb[:], sq_st[l_][:],
                                                      start=False, stop=True),
                 wait=[(SE, ptsq)], inc=PE)
            tok_stats = PE.n

            emit("vector", lambda e: e.tensor_scalar(rowt(0, 512), PS_A[0:1, :],
                                                     1.0 / H, None, OP.mult),
                 wait=[(PE, tok_stats)], inc=VE)
            emit("vector", lambda e: e.tensor_scalar(rowt(1, 512), PS_B[0:1, :],
                                                     1.0 / H, None, OP.mult), inc=VE)
            emit("vector", lambda e: e.tensor_mul(rowt(2, 512), rowt(0, 512), rowt(0, 512)), inc=VE)
            emit("vector", lambda e: e.tensor_sub(rowt(3, 512), rowt(1, 512), rowt(2, 512)), inc=VE)
            emit("vector", lambda e: e.tensor_scalar(rowt(1, 512), rowt(3, 512), EPS_LN,
                                                     None, OP.add), inc=VE)
            t8 = emit("vector", lambda e: e.reciprocal(rowt(2, 512), rowt(1, 512)), inc=VE)
            t9 = emit("scalar", lambda e: e.activation(rowt(1, 512), rowt(2, 512), AF.Sqrt),
                      wait=[(VE, t8)], inc=SE)
            t10 = emit("vector", lambda e: e.tensor_mul(rowt(2, 512), rowt(0, 512), rowt(1, 512)),
                       wait=[(SE, t9)], inc=VE)
            wbc = _w(tok_wo_free[0], tok_wo_free[1], (VE, t10))
            rb_ap = PS_SC[:, 1024:1536] if ct == 0 else PS_A[:, :]
            mrb_ap = PS_SC[:, 1536:2048] if ct == 0 else PS_B[:, :]
            emit("tensor", lambda e, a=rb_ap: e.matmul(a, ones1_sb[:], rowt(1, 512),
                                                       start=True, stop=True), wait=wbc)
            emit("tensor", lambda e, a=mrb_ap: e.matmul(a, ones1_sb[:], rowt(2, 512),
                                                        start=True, stop=True), inc=PE)
            tok_bc = PE.n

            for fc in range(NFC):
                par = fc % 3
                sl = fc % 2
                smld = SM("xld%d" % par)
                tld = (smld, emit("sync", lambda e, o=p2x[par], f=fc, c=ct: e.dma_start(
                    out=o, in_=xres_sc[f * 128:(f + 1) * 128, c * 512:(c + 1) * 512]),
                    wait=_w(use_ld[par], use_xres[sl], use_ysc[par], use_tyb[par]),
                    inc=smld, step=16))
                tt1 = emit("vector", lambda e, i=p2x[par], o=p2t[par], a=rb_ap: e.tensor_mul(
                    o, i, a),
                    wait=_w(tld, (PE, tok_bc)), inc=VE)
                ty = emit("vector", lambda e, o=p2x[par], i=p2t[par], a=mrb_ap: e.tensor_sub(
                    o, i, a), wait=_w(use_ysc[par], use_y[par]), inc=VE)
                use_ld[par] = (VE, ty)
                use_xres[sl] = tld
                tyb = emit("scalar", lambda e, i=p2x[par], o=p2yb[par]: e.copy(o, i),
                           wait=_w((VE, ty), use_y[par]), inc=SE)
                use_tyb[par] = (SE, tyb)
                smy, smyb = SM("yst%d" % par), SM("ybst%d" % par)
                tsy = emit("scalar", lambda e, i=p2x[par], f=fc, c=ct: e.dma_start(
                    out=y_sc[f * 128:(f + 1) * 128, c * 512:(c + 1) * 512], in_=i),
                    wait=[(VE, ty)], inc=smy, step=16)
                ts2 = emit("scalar", lambda e, i=p2yb[par], f=fc, c=ct: e.dma_start(
                    out=ybf_sc[f * 128:(f + 1) * 128, c * 512:(c + 1) * 512], in_=i),
                    wait=[(SE, tyb)], inc=smyb, step=16)
                use_y[par] = (smyb, ts2)
                use_ysc[par] = (smy, tsy)
            if ct == 0:
                tok_c_p2t0 = (VE, VE.n)
                y_done0 = [(SM("ybst%d" % i2), SM("ybst%d" % i2).n) for i2 in range(3)]
        y_done1 = [(SM("ybst%d" % i2), SM("ybst%d" % i2).n) for i2 in range(3)]
        tok_c_pass2 = (VE, VE.n)
        tok_c_pe = (PE, PE.n)

        y_tile = bfv(0, 16384)
        h_buf = bfv(16384, 44032)
        gu = [bfv(60416, 2048), bfv(62464, 2048), bfv(64512, 2048), bfv(66560, 2048)]
        tmp_sg = bfv(68608, 512)
        tmp_us = bfv(69120, 512)
        dslab = [bfv(60416, 5504), bfv(65920, 5504)]
        xbf2 = [bfv(71424, 512), bfv(71936, 512)]
        sq2 = [bfv(72448, 512), bfv(72960, 512)]
        y32b = [f32v(0, 512), f32v(512, 512)]
        xr2 = [f32v(1024, 512), f32v(1536, 512)]
        d2x = [f32v(2048, 512), f32v(2560, 512), f32v(3072, 512)]
        d2t = [f32v(3584, 512), f32v(4096, 512), f32v(4608, 512)]

        tok_ytile_user = None
        use_y32 = [None, None]
        use_xr2 = [None, None]
        use_ds = [None, None]
        prev_q = {0: tok_wo_free[0], 1: tok_wo_free[1], 2: tok_c_p2t0, 3: tok_c_p2t0}
        tok_d_p2 = None
        use_ld2 = [None, None, None]
        use_out = [None, None, None]
        for tt in range(2):
            w0 = _w(*(y_done0 if tt == 0 else y_done1)) + _w(
                tok_ytile_user, (tok_c_pe if tt == 0 else None))
            for g in range(4):
                emit("gpsimd", lambda e, gg=g, t_=tt: e.dma_start(
                    out=y_tile[:, gg * 4096:(gg + 1) * 4096].rearrange("p (kc t) -> p kc t", kc=8),
                    in_=ybf_sc.rearrange("(kc p) t -> p kc t", p=128)[:, gg * 8:(gg + 1) * 8,
                                                                     t_ * 512:(t_ + 1) * 512]),
                    wait=(w0 if g == 0 else ()), inc=SM("ytile"), step=16)
            tok_ytile = (SM("ytile"), SM("ytile").n)

            use_gu = [None, None, None, None]
            tok_guevac = {}
            tok_th_prev = None
            for ic in range(NIC):
                sl = ic % 2
                toks_ld = [None] * 4
                for m in range(2):
                    for half in range(2):
                        bi = m * 2 + half
                        src = d["wg_t"] if m == 0 else d["wu_t"]
                        sm = SM("gu%d" % bi)
                        toks_ld[bi] = (sm, emit("gpsimd", lambda e, s_=gu[bi], a=src, i=ic, hf=half:
                                                e.dma_start(
                                                    out=s_.rearrange("p (kc m_) -> p kc m_", kc=16),
                                                    in_=a[i, :, hf * 16:(hf + 1) * 16, :]),
                                                wait=_w(use_gu[bi])
                                                + (_w(use_ds[0], use_ds[1],
                                                      (tok_bend_pe if tt == 0 else None))
                                                   if ic == 0 else []),
                                                inc=sm, step=16))
                qg = PS_SC[:, (2 * sl) * 512:(2 * sl + 1) * 512]
                qu = PS_SC[:, (2 * sl + 1) * 512:(2 * sl + 2) * 512]
                tok_g = tok_u = None
                for m in range(2):
                    ps = qg if m == 0 else qu
                    tmm = None
                    for half in range(2):
                        bi = m * 2 + half
                        for k in range(16):
                            kc = half * 16 + k
                            tmm = emit("tensor", lambda e, p=ps, s_=gu[bi], kk=k, c=kc,
                                       st=(kc == 0), sp=(kc == NFC - 1):
                                       e.matmul(p, s_[:, kk * 128:(kk + 1) * 128],
                                                y_tile[:, c * 512:(c + 1) * 512],
                                                start=st, stop=sp),
                                       wait=(_w(toks_ld[bi], tok_ytile,
                                                tok_guevac.get((sl, m)) or prev_q[2 * sl + m])
                                             if kc == half * 16 else ()),
                                       inc=(PE if k == 15 else None))
                        use_gu[bi] = (PE, tmm)
                    if m == 0:
                        tok_g = tmm
                    else:
                        tok_u = tmm
                tsg = emit("scalar", lambda e, p=qg, i=ic: e.activation(
                    tmp_sg[:], p, AF.Silu, scale=sg_sb[:, i:i + 1]),
                    wait=_w((PE, tok_g), tok_th_prev), inc=SE)
                tus = emit("vector", lambda e, p=qu, i=ic: e.tensor_scalar_mul(
                    tmp_us[:], p, su_sb[:, i:i + 1]),
                    wait=[(PE, tok_u)], inc=VE)
                if DEBUG_SPILL and ic == 0:
                    tdbg1 = emit("sync", lambda e, t_=tt: e.dma_start(out=dbg_sg[t_], in_=tmp_sg[:]),
                                 wait=[(SE, tsg)], inc=SM("dbg"), step=16)
                    tdbg2 = emit("sync", lambda e, t_=tt: e.dma_start(out=dbg_us[t_], in_=tmp_us[:]),
                                 wait=[(VE, tus)], inc=SM("dbg"), step=16)
                    tok_th_prev = (SM("dbg"), tdbg2)
                th = emit("vector", lambda e, i=ic: e.tensor_mul(
                    h_buf[:, i * 512:(i + 1) * 512], tmp_sg[:], tmp_us[:]),
                    wait=[(SE, tsg), (VE, tus)] + (_w(tok_ytile_user) if ic == 0 else [])
                    + (_w(tok_th_prev) if (DEBUG_SPILL and ic == 0) else []),
                    inc=VE)
                tok_th_prev = (VE, th)
                tok_guevac[(sl, 0)] = (SE, tsg)
                tok_guevac[(sl, 1)] = (VE, tus)
            tok_h_done = (VE, VE.n)
            if DEBUG_SPILL:
                emit("sync", lambda e, t_=tt: e.dma_start(out=dbg_h[t_], in_=h_buf),
                     wait=[tok_h_done], inc=SM("dbg"), step=16)

            tok_dfree = {}
            tok_stats2 = None
            pend_stats2 = None
            tok_stat_pe = [None, None]
            for fc in range(NFC):
                sl = fc % 2
                toks_ld = [None, None]
                for half in range(2):
                    sm = SM("ds%d" % half)
                    toks_ld[half] = (sm, emit("gpsimd", lambda e, s_=dslab[half], f=fc, hf=half:
                                              e.dma_start(
                                                  out=s_.rearrange("p (kc m_) -> p kc m_", kc=43),
                                                  in_=d["wd_t"][f, :, hf * 43:(hf + 1) * 43, :]),
                                              wait=_w(use_ds[half])
                                              + (_w(use_gu[0], use_gu[1], use_gu[2], use_gu[3],
                                                    tok_h_done) if fc == 0 else []),
                                              inc=sm, step=16))
                sm32 = SM("y32%d" % sl)
                ty32 = (sm32, emit("sync", lambda e, o=y32b[sl], f=fc, t_=tt: e.dma_start(
                    out=o, in_=y_sc[f * 128:(f + 1) * 128, t_ * 512:(t_ + 1) * 512]),
                    wait=_w(use_y32[sl])
                    + (_w(tok_c_pass2) if (tt == 0 and fc < 2) else []),
                    inc=sm32, step=16))
                ps = PS_SC[:, sl * 512:(sl + 1) * 512]
                tmm = None
                for half in range(2):
                    for k in range(43):
                        kc = half * 43 + k
                        tmm = emit("tensor", lambda e, p=ps, s_=dslab[half], kk=k, c=kc,
                                   st=(kc == 0), sp=(kc == NIC - 1):
                                   e.matmul(p, s_[:, kk * 128:(kk + 1) * 128],
                                            h_buf[:, c * 512:(c + 1) * 512], start=st, stop=sp),
                                   wait=(_w(toks_ld[half], tok_dfree.get(sl),
                                            tok_h_done) if kc == half * 43 else ()),
                                   inc=(PE if k == 42 else None))
                    use_ds[half] = (PE, tmm)
                if pend_stats2 is not None:
                    psl, ptbf, ptsq, pfc = pend_stats2
                    wst = (_w(tok_c_pass2 if tt == 0 else tok_d_p2)
                           if pfc == 0 else [])
                    emit("tensor", lambda e, l_=psl, st=(pfc == 0): e.matmul(
                        PS_A[0:1, :], onesb_sb[:], xbf2[l_][:], start=st, stop=False),
                        wait=[(VE, ptbf)] + wst)
                    tss = emit("tensor", lambda e, l_=psl, st=(pfc == 0): e.matmul(
                        PS_B[0:1, :], onesb_sb[:], sq2[l_][:], start=st, stop=False),
                        wait=[(SE, ptsq)] + wst, inc=PE)
                    tok_stat_pe[psl] = (PE, tss)
                txr = emit("vector", lambda e, o=xr2[sl], p=ps, f=fc, b_=y32b[sl]:
                           e.scalar_tensor_tensor(o, p, sd_sb[:, f:f + 1], b_, OP.mult, OP.add),
                           wait=_w((PE, tmm), ty32, use_xr2[sl]), inc=VE)
                tok_dfree[sl] = (VE, txr)
                use_y32[sl] = (VE, txr)
                tbf = emit("vector", lambda e, i=xr2[sl], o=xbf2[sl]: e.tensor_copy(o[:], i),
                           wait=_w(tok_stat_pe[sl]), inc=VE)
                tsq = emit("scalar", lambda e, i=xr2[sl], o=sq2[sl]: e.activation(o[:], i, AF.Square),
                           wait=_w((VE, txr), tok_stat_pe[sl]), inc=SE)
                pend_stats2 = (sl, tbf, tsq, fc)
                smxs = SM("xst%d" % sl)
                tsx = emit("sync", lambda e, i=xr2[sl], f=fc, t_=tt: e.dma_start(
                    out=xres_sc[f * 128:(f + 1) * 128, t_ * 512:(t_ + 1) * 512], in_=i),
                    wait=[(VE, txr)], inc=smxs, step=16)
                use_xr2[sl] = (smxs, tsx)
            psl, ptbf, ptsq, pfc = pend_stats2
            emit("tensor", lambda e, l_=psl: e.matmul(PS_A[0:1, :], onesb_sb[:], xbf2[l_][:],
                                                      start=False, stop=True),
                 wait=[(VE, ptbf)])
            emit("tensor", lambda e, l_=psl: e.matmul(PS_B[0:1, :], onesb_sb[:], sq2[l_][:],
                                                      start=False, stop=True),
                 wait=[(SE, ptsq)], inc=PE)
            tok_stats2 = (PE, PE.n)

            emit("vector", lambda e: e.tensor_scalar(rowt(0, 512), PS_A[0:1, :], 1.0 / H,
                                                     None, OP.mult),
                 wait=[(PE, PE.n)], inc=VE)
            emit("vector", lambda e: e.tensor_scalar(rowt(1, 512), PS_B[0:1, :], 1.0 / H,
                                                     None, OP.mult), inc=VE)
            emit("vector", lambda e: e.tensor_mul(rowt(2, 512), rowt(0, 512), rowt(0, 512)), inc=VE)
            emit("vector", lambda e: e.tensor_sub(rowt(3, 512), rowt(1, 512), rowt(2, 512)), inc=VE)
            emit("vector", lambda e: e.tensor_scalar(rowt(1, 512), rowt(3, 512), EPS_LN,
                                                     None, OP.add), inc=VE)
            t8 = emit("vector", lambda e: e.reciprocal(rowt(2, 512), rowt(1, 512)), inc=VE)
            t9 = emit("scalar", lambda e: e.activation(rowt(1, 512), rowt(2, 512), AF.Sqrt),
                      wait=[(VE, t8)], inc=SE)
            t10 = emit("vector", lambda e: e.tensor_mul(rowt(2, 512), rowt(0, 512), rowt(1, 512)),
                       wait=[(SE, t9)], inc=VE)
            wbc = _w(tok_dfree.get(0), tok_dfree.get(1), (VE, t10))
            emit("tensor", lambda e: e.matmul(PS_A[:, :], ones1_sb[:], rowt(1, 512),
                                              start=True, stop=True), wait=wbc)
            emit("tensor", lambda e: e.matmul(PS_B[:, :], ones1_sb[:], rowt(2, 512),
                                              start=True, stop=True), inc=PE)
            tok_bc2 = PE.n

            for fc in range(NFC):
                par = fc % 3
                sl = fc % 2
                smld = SM("xld%d" % par)
                tld = (smld, emit("sync", lambda e, o=d2x[par], f=fc, t_=tt: e.dma_start(
                    out=o, in_=xres_sc[f * 128:(f + 1) * 128, t_ * 512:(t_ + 1) * 512]),
                    wait=_w(use_ld2[par], use_xr2[sl], use_out[par]), inc=smld, step=16))
                tt1 = emit("vector", lambda e, i=d2x[par], o=d2t[par]: e.tensor_mul(
                    o, i, PS_A[:, :]),
                    wait=_w(tld, (PE, tok_bc2)), inc=VE)
                tout = emit("vector", lambda e, o=d2x[par], i=d2t[par]: e.tensor_sub(
                    o, i, PS_B[:, :]), wait=_w(use_out[par]), inc=VE)
                use_ld2[par] = (VE, tout)
                use_xr2[sl] = tld
                smo = SM("ost%d" % par)
                tso = emit("scalar", lambda e, i=d2x[par], f=fc, t_=tt: e.dma_start(
                    out=outT[f * 128:(f + 1) * 128, t_ * 512:(t_ + 1) * 512], in_=i),
                    wait=[(VE, tout)], inc=smo, step=16)
                use_out[par] = (smo, tso)
            tok_ytile_user = (PE, PE.n)
            prev_q = {0: tok_dfree.get(0), 1: tok_dfree.get(1),
                      2: tok_guevac.get((1, 0)), 3: tok_guevac.get((1, 1))}
            tok_d_p2 = (VE, VE.n)

        E.ops["sync"].append((None, [(SM("ost%d" % i2), SM("ost%d" % i2).n)
                                     for i2 in range(3)], None))

        @block.tensor
        def _(t):
            E.replay("tensor", t)

        @block.vector
        def _(v):
            E.replay("vector", v)

        @block.scalar
        def _(s):
            E.replay("scalar", s)

        @block.gpsimd
        def _(g):
            E.replay("gpsimd", g)

        @block.sync
        def _(sy):
            E.replay("sync", sy)

    return nc


_PERM_ROPE = [2 * j for j in range(32)] + [2 * j + 1 for j in range(32)]


def _rope_tabs(npos):
    freqs = 1.0 / 10000.0 ** (np.arange(0, DR, 2, dtype=np.float32) / DR)
    t = np.arange(npos, dtype=np.float32)
    ang = np.outer(t, freqs)
    return np.cos(ang).astype(np.float32), np.sin(ang).astype(np.float32)


def _prep_shared(inputs):
    wq_a = np.asarray(inputs["L0_attn_wq_a"], np.float32)
    wq_b = np.asarray(inputs["L0_attn_wq_b"], np.float32)
    wkv_a = np.asarray(inputs["L0_attn_wkv_a"], np.float32)
    wkv_b = np.asarray(inputs["L0_attn_wkv_b"], np.float32)
    wo = np.asarray(inputs["L0_attn_wo"], np.float32)
    w_gate = np.asarray(inputs["L0_ffn_w_gate"], np.int32)
    w_up = np.asarray(inputs["L0_ffn_w_up"], np.int32)
    w_down = np.asarray(inputs["L0_ffn_w_down"], np.int32)

    def tile_w(w, nmc, nkc, mw=128):
        return np.ascontiguousarray(w.reshape(nmc, mw, nkc, 128).transpose(0, 3, 2, 1))

    def to_i8(a):
        return np.ascontiguousarray(a.view(np.int8)[..., 0::4])

    sh = {}
    sh["wqa_t"] = tile_w(wq_a, NKC_QL, NFC)
    rows = []
    for h in range(NH):
        base = h * 192
        rows.extend(range(base, base + 128))
        rows.extend(base + 128 + p for p in _PERM_ROPE)
    wqb_dev = wq_b[np.asarray(rows)]
    sh["wqb_t"] = np.ascontiguousarray(
        wqb_dev.reshape(NH, 192, NKC_QL, 128).transpose(0, 3, 2, 1))
    sh["wkva_c"] = tile_w(wkv_a[:KVL], NCC, NFC)
    wkva_pe = wkv_a[KVL + np.asarray(_PERM_ROPE)]
    sh["wkva_pe"] = np.ascontiguousarray(wkva_pe.reshape(64, NFC, 128).transpose(2, 1, 0))
    kvb = wkv_b.reshape(NH, 256, NCC, 128)
    sh["wkvb_k"] = np.ascontiguousarray(kvb[:, :128].transpose(0, 3, 2, 1))
    sh["wkvb_v"] = np.ascontiguousarray(kvb[:, 128:].transpose(2, 3, 0, 1))
    sh["wo_t"] = tile_w(wo, NFC, NFC)
    sh["wg_t"] = to_i8(tile_w(w_gate, NIC, NFC))
    sh["wu_t"] = to_i8(tile_w(w_up, NIC, NFC))
    sh["wd_t"] = to_i8(tile_w(w_down, NFC, NIC))
    sh["qnw_t"] = np.ascontiguousarray(
        np.asarray(inputs["L0_attn_q_norm_w"], np.float32).reshape(NKC_QL, 128).T)
    sh["kvnw_t"] = np.ascontiguousarray(
        np.asarray(inputs["L0_attn_kv_norm_w"], np.float32).reshape(NCC, 128).T)
    sh["sg_t"] = np.ascontiguousarray(
        np.asarray(inputs["L0_ffn_s_gate"], np.float32).reshape(NIC, 128).T)
    sh["su_t"] = np.ascontiguousarray(
        np.asarray(inputs["L0_ffn_s_up"], np.float32).reshape(NIC, 128).T)
    sh["sd_t"] = np.ascontiguousarray(
        np.asarray(inputs["L0_ffn_s_down"], np.float32).reshape(NFC, 128).T)
    sh["ident"] = np.eye(128, dtype=BF16NP)
    sh["onesb"] = np.ones((128, 1), dtype=BF16NP)
    sh["ones1"] = np.ones((1, 128), dtype=np.float32)
    cosk, sink = _rope_tabs(S)
    sh["coskp"] = np.ascontiguousarray(np.concatenate([cosk.T, cosk.T], axis=0))
    sh["sinkp"] = np.ascontiguousarray(np.concatenate([-sink.T, sink.T], axis=0))
    return sh


def _prep_core(inputs, b, hpar):
    x = np.asarray(inputs["L0_attn_x"], np.float32)[b]
    xT = np.ascontiguousarray(x.T)
    qt = QTILES[hpar]
    own_cols = np.concatenate([np.arange(g * 128, (g + 1) * 128) for g in qt])
    pc = {}
    pc["xT_all"] = xT
    pc["xT_own"] = np.ascontiguousarray(xT[:, own_cols])
    m = np.zeros((128, NSLOT * 2 * 128), np.float32)
    for s in range(NSLOT):
        g = qt[s]
        for j in range(2):
            c = KS[s] - 2 + j
            keys = np.arange(c * 128, (c + 1) * 128)
            qrows = g * 128 + np.arange(128)
            m[:, (s * 2 + j) * 128:(s * 2 + j + 1) * 128] = np.where(
                keys[None, :] <= qrows[:, None], 0.0, MASKVAL)
    pc["mask2"] = m.astype(BF16NP)
    cosk, sink = _rope_tabs(S)
    cq = cosk[own_cols].T
    sq = sink[own_cols].T
    pc["ropeC"] = np.ascontiguousarray(
        (np.concatenate([cq, cq, cq, cq], axis=0) * SCALE).astype(np.float32))
    pc["ropeS"] = np.ascontiguousarray(
        (np.concatenate([-sq, sq, -sq, sq], axis=0) * SCALE).astype(np.float32))
    return pc, own_cols


def kernel(**inputs):
    nc = build_program()
    sh = _prep_shared(inputs)
    in_maps = []
    cols = []
    for core in range(8):
        b, hpar = core // 2, core % 2
        pc, own_cols = _prep_core(inputs, b, hpar)
        m = dict(sh)
        m.update(pc)
        in_maps.append(m)
        cols.append(own_cols)
    res = run_bass_kernel_spmd(nc, in_maps, core_ids=list(range(8)))
    out = np.empty((B, S, H), np.float32)
    for core in range(8):
        b = core // 2
        o = res.results[core]["outT"]
        out[b, cols[core], :] = np.ascontiguousarray(o).T
    return out


# version 08
# speedup vs baseline: 1.0969x; 1.0342x over previous
import math
import numpy as np

DEBUG_SPILL = False
import ml_dtypes
import concourse.bass as bass
import concourse.mybir as mybir
from concourse.bass_utils import run_bass_kernel_spmd

F32 = mybir.dt.float32
BF16 = mybir.dt.bfloat16
I8 = mybir.dt.int8
AF = mybir.ActivationFunctionType
OP = mybir.AluOpType
AX = mybir.AxisListType
BF16NP = ml_dtypes.bfloat16

B, S, H = 4, 2048, 4096
NH, DN, DR, DV = 32, 128, 64, 128
QL, KVL = 1536, 512
INTER = 11008
NIC = INTER // 128
NFC = H // 128
NKC_QL = QL // 128
NCC = KVL // 128
EPS_RMS, EPS_LN = 1e-6, 1e-5
SCALE = 1.0 / math.sqrt(DN + DR)
TQ = 1024
NSLOT = 8
MASKVAL = -1e30

QTILES = [[0, 3, 4, 7, 8, 11, 12, 15], [1, 2, 5, 6, 9, 10, 13, 14]]
KS = [2 * (s + 1) for s in range(NSLOT)]

ENGINES = ("tensor", "vector", "scalar", "gpsimd", "sync")


class Sem:
    def __init__(self, name):
        self.name = name
        self.n = 0
        self.h = None


class Emitter:
    def __init__(self):
        self.ops = {e: [] for e in ENGINES}

    def emit(self, eng, fn, wait=(), inc=None, step=1):
        w = [(s, v) for (s, v) in wait if s is not None and v and v > 0]
        tok = None
        if inc is not None:
            inc.n += step
            tok = inc.n
        self.ops[eng].append((fn, w, (inc, step) if inc is not None else None))
        return tok

    def barrier(self, sems):
        snap = [(s, s.n) for s in sems if s.n > 0]
        for e in ENGINES:
            self.ops[e].append((None, snap, None))

    def replay(self, eng_name, eng):
        waited = {}
        for fn, waits, inc in self.ops[eng_name]:
            for sem, val in waits:
                if waited.get(sem, -1) < val:
                    eng.wait_ge(sem.h, val)
                    waited[sem] = val
            if fn is not None:
                ins = fn(eng)
                if inc is not None:
                    ins.then_inc(inc[0].h, inc[1])


def _w(*tokens):
    return [t for t in tokens if t is not None]


def build_program():
    nc = bass.Bass("TRN2", target_bir_lowering=False)
    E = Emitter()
    PE, VE, SE = Sem("PE"), Sem("VE"), Sem("SE")
    emit = E.emit

    d = {}
    def din(name, shape, dt):
        d[name] = nc.dram_tensor(name, shape, dt, kind="ExternalInput").ap()
    din("xT_all", [H, S], F32)
    din("xT_own", [H, TQ], F32)
    din("wqa_t", [NKC_QL, 128, NFC, 128], F32)
    din("wqb_t", [NH, 128, NKC_QL, 128], F32)
    din("wqbr_t", [NH // 2, 128, NKC_QL, 128], F32)
    din("wkva_c", [NCC, 128, NFC, 128], F32)
    din("wkva_pe", [128, NFC, 64], F32)
    din("wkvb_k", [NH, 128, NCC, 128], F32)
    din("wkvb_v", [NCC, 128, NH, 128], F32)
    din("wo_t", [NFC, 128, NFC, 128], F32)
    din("wg_t", [NIC, 128, NFC, 128], I8)
    din("wu_t", [NIC, 128, NFC, 128], I8)
    din("wd_t", [NFC, 128, NIC, 128], I8)
    din("qnw_t", [128, NKC_QL], F32)
    din("kvnw_t", [128, NCC], F32)
    din("sg_t", [128, NIC], F32)
    din("su_t", [128, NIC], F32)
    din("sd_t", [128, NFC], F32)
    din("ident", [128, 128], BF16)
    din("onesb", [128, 1], BF16)
    din("ones1", [1, 128], F32)
    din("mask2", [128, NSLOT * 2 * 128], BF16)
    din("ropeC", [128, TQ], F32)
    din("ropeS", [128, TQ], F32)
    din("coskp", [64, S], F32)
    din("sinkp", [64, S], F32)

    sck = "ExternalOutput" if DEBUG_SPILL else "Internal"
    outT = nc.dram_tensor("outT", [H, TQ], F32, kind="ExternalOutput").ap()
    attn_sc = nc.dram_tensor("attn_sc", [H, TQ], BF16, kind=sck).ap()
    xres_sc = nc.dram_tensor("xres_sc", [H, TQ], F32, kind=sck).ap()
    y_sc = nc.dram_tensor("y_sc", [H, TQ], F32, kind=sck).ap()
    ybf_sc = nc.dram_tensor("ybf_sc", [H, TQ], BF16, kind=sck).ap()
    if DEBUG_SPILL:
        dbg_qn = nc.dram_tensor("dbg_qn", [128, NKC_QL * TQ], BF16, kind="ExternalOutput").ap()
        dbg_kvn = nc.dram_tensor("dbg_kvn", [128, NCC * S], BF16, kind="ExternalOutput").ap()
        dbg_kpe = nc.dram_tensor("dbg_kpe", [64, S], BF16, kind="ExternalOutput").ap()
        dbg_h = nc.dram_tensor("dbg_h", [2, 128, NIC * 512], BF16, kind="ExternalOutput").ap()
        dbg_x1 = nc.dram_tensor("dbg_x1", [H, TQ], F32, kind="ExternalOutput").ap()
        dbg_sg = nc.dram_tensor("dbg_sg", [2, 128, 512], BF16, kind="ExternalOutput").ap()
        dbg_us = nc.dram_tensor("dbg_us", [2, 128, 512], BF16, kind="ExternalOutput").ap()

    import contextlib
    with contextlib.ExitStack() as stack:
        ec = stack.enter_context
        ident_sb = ec(nc.sbuf_tensor("ident_sb", [128, 128], BF16))
        onesb_sb = ec(nc.sbuf_tensor("onesb_sb", [128, 1], BF16))
        ones1_sb = ec(nc.sbuf_tensor("ones1_sb", [1, 128], F32))
        qnw_sb = ec(nc.sbuf_tensor("qnw_sb", [128, NKC_QL], F32))
        kvnw_sb = ec(nc.sbuf_tensor("kvnw_sb", [128, NCC], F32))
        sg_sb = ec(nc.sbuf_tensor("sg_sb", [128, NIC], F32))
        su_sb = ec(nc.sbuf_tensor("su_sb", [128, NIC], F32))
        sd_sb = ec(nc.sbuf_tensor("sd_sb", [128, NFC], F32))
        rows = ec(nc.sbuf_tensor("rows", [1, 4096], F32))
        zbuf = ec(nc.sbuf_tensor("zbuf", [128, 64], F32))
        pbf = ec(nc.sbuf_tensor("pbf", [128, 73472], BF16))
        pf32 = ec(nc.sbuf_tensor("pf32", [128, 7168], F32))
        PS_SC = ec(nc.psum_tensor("PS_SC", [128, 2048], F32))
        PS_PTA = ec(nc.psum_tensor("PS_PTA", [128, 1024], BF16))
        PS_PTB = ec(nc.psum_tensor("PS_PTB", [128, 1024], BF16))
        PS_A = ec(nc.psum_tensor("PS_A", [128, 512], F32))
        PS_B = ec(nc.psum_tensor("PS_B", [128, 512], F32))
        PE.h = ec(nc.semaphore("sPE"))
        VE.h = ec(nc.semaphore("sVE"))
        SE.h = ec(nc.semaphore("sSE"))
        block = ec(nc.Block())

        sems = {}
        def SM(name):
            if name not in sems:
                s_ = Sem(name)
                s_.h = ec(nc.semaphore("sm_" + name))
                sems[name] = s_
            return sems[name]
        def allsems():
            return [PE, VE, SE] + list(sems.values())

        def bfv(col, n):
            return pbf[:, col:col + n]
        def f32v(col, n):
            return pf32[:, col:col + n]
        def rowt(r, n):
            return rows[0:1, r * 1024:r * 1024 + n]

        qn = bfv(45568, NKC_QL * TQ)
        kvn = bfv(57856, NCC * S)
        kpe = bfv(66048, S)[0:64, :]
        mask2_sb = bfv(68096, NSLOT * 2 * 128)

        for (dst, src) in [
            (ident_sb[:], d["ident"]), (onesb_sb[:], d["onesb"]), (ones1_sb[:], d["ones1"]),
            (qnw_sb[:], d["qnw_t"]), (kvnw_sb[:], d["kvnw_t"]),
            (sg_sb[:], d["sg_t"]), (su_sb[:], d["su_t"]), (sd_sb[:], d["sd_t"]),
            (mask2_sb[:], d["mask2"]),
        ]:
            emit("sync", lambda e, a=dst, b=src: e.dma_start(out=a, in_=b), inc=SM("consts"), step=16)
        tok_consts = (SM("consts"), SM("consts").n)

        xab = [bfv(0, 16384), bfv(25088, 16384)]
        aslab = [bfv(16384, 4096), bfv(20480, 4096), bfv(41472, 4096)]
        sqst = bfv(24576, 512)
        cosk = pf32[0:64, 2048:4096]
        sink = pf32[0:64, 4096:6144]
        krA = pf32[:, 6144:6656]
        krB = pf32[:, 6656:7168]

        emit("sync", lambda e: e.dma_start(out=cosk, in_=d["coskp"]), inc=SM("consts"), step=16)
        emit("sync", lambda e: e.dma_start(out=sink, in_=d["sinkp"]), inc=SM("consts"), step=16)
        tok_consts = (SM("consts"), SM("consts").n)

        use_xa = [None, None]
        use_slab = [None, None, None]
        use_sq = None
        ps_free = {}

        slab_jobs = []
        for _q in range(4):
            for _mc in range(NCC):
                slab_jobs.append((d["wkva_c"][_mc], 128))
            slab_jobs.append((d["wkva_pe"], 64))
        for _q in range(2):
            for _mc in range(NKC_QL):
                slab_jobs.append((d["wqa_t"][_mc], 128))
        slab_tok = [None] * len(slab_jobs)
        slab_emitted = [0]

        def slab_emit_next():
            j = slab_emitted[0]
            if j >= len(slab_jobs):
                return
            b = j % 3
            srcj, mw = slab_jobs[j]
            sm = SM("aslab%d" % b)
            slab_tok[j] = (sm, emit("gpsimd", lambda e, s_=aslab[b], a=srcj, m_=mw: e.dma_start(
                out=s_.rearrange("p (kc m) -> p kc m", kc=NFC)[:, :, 0:m_], in_=a),
                wait=_w(use_slab[b]), inc=sm, step=16))
            slab_emitted[0] += 1

        def proj_quarter(xa, tok_xa, nmc, dst_cols_fn, nw_sb, nfeat, pe_chunk, jbase, q4):
            nonlocal use_sq
            for mc in range(nmc + (1 if pe_chunk else 0)):
                is_pe = pe_chunk and mc == nmc
                j = jbase + mc
                sl = j % 3
                slab = aslab[sl]
                mwid = 64 if is_pe else 128
                while slab_emitted[0] <= j:
                    slab_emit_next()
                tok_slab = slab_tok[j]
                if is_pe:
                    ps = PS_SC[0:64, 1024:1536]
                    pskey = 2
                else:
                    ps = PS_SC[:, sl * 512:(sl + 1) * 512]
                    pskey = sl
                wmm = _w(ps_free.get(pskey), tok_xa, tok_slab)
                tmm = None
                for fc in range(NFC):
                    tmm = emit("tensor", lambda e, p=ps, s_=slab, f=fc, mw=mwid,
                               st=(fc == 0), sp=(fc == NFC - 1):
                               e.matmul(p, s_[:, f * 128:f * 128 + mw],
                                        xa[:, f * 512:(f + 1) * 512], start=st, stop=sp),
                               wait=(wmm if fc == 0 else ()),
                               inc=(PE if fc == NFC - 1 else None))
                use_slab[sl] = (PE, tmm)
                slab_emit_next()
                if not is_pe:
                    st_f32 = f32v(mc * 512, 512)
                    tok_ev = emit("vector", lambda e, o=st_f32, p=ps: e.tensor_copy(o, p),
                                  wait=[(PE, tmm)], inc=VE)
                    ps_free[pskey] = (VE, tok_ev)
                    tok_sq = emit("scalar", lambda e, i=st_f32: e.activation(sqst[:], i, AF.Square),
                                  wait=_w((VE, tok_ev), use_sq), inc=SE)
                    tok_ss = emit("tensor", lambda e, st=(mc == 0), sp=(mc == nmc - 1):
                                  e.matmul(PS_A[0:1, :], onesb_sb[:], sqst[:], start=st, stop=sp),
                                  wait=[(SE, tok_sq)] + (_w(ps_free.get("ss")) if mc == 0 else []),
                                  inc=PE)
                    use_sq = (PE, tok_ss)
                else:
                    tA = emit("vector", lambda e: e.tensor_copy(krA[0:64, :], PS_SC[0:64, 1024:1536]),
                              wait=[(PE, tmm)], inc=VE)
                    ps_free[pskey] = (VE, tA)
                    tb1 = emit("scalar", lambda e: e.copy(krB[0:32, :], krA[32:64, :]),
                               wait=[(VE, tA)], inc=SE)
                    tb2 = emit("scalar", lambda e: e.copy(krB[32:64, :], krA[0:32, :]),
                               wait=[(VE, tA)], inc=SE)
                    ck = cosk[:, q4 * 512:(q4 + 1) * 512]
                    sk = sink[:, q4 * 512:(q4 + 1) * 512]
                    emit("vector", lambda e, c=ck: e.tensor_mul(krA[64:128, :], krA[0:64, :], c),
                         wait=_w((SE, tb2), tok_consts), inc=VE)
                    emit("vector", lambda e, s_=sk: e.tensor_mul(krB[64:128, :], krB[0:64, :], s_),
                         inc=VE)
                    emit("vector", lambda e, c=q4: e.tensor_add(
                        kpe[:, c * 512:(c + 1) * 512], krA[64:128, :], krB[64:128, :]), inc=VE)
            tr1 = emit("vector", lambda e, nf=nfeat: e.tensor_scalar(
                rowt(0, 512), PS_A[0:1, :], 1.0 / nf, EPS_RMS, OP.mult, OP.add),
                wait=[(PE, PE.n)], inc=VE)
            ps_free["ss"] = (VE, tr1)
            emit("vector", lambda e: e.reciprocal(rowt(1, 512), rowt(0, 512)), inc=VE)
            tr3 = emit("scalar", lambda e: e.activation(rowt(2, 512), rowt(1, 512), AF.Sqrt),
                       wait=[(VE, VE.n)], inc=SE)
            tbc = emit("tensor", lambda e: e.matmul(PS_SC[:, 1536:2048], ones1_sb[:], rowt(2, 512),
                                                    start=True, stop=True),
                       wait=[(SE, tr3)] + _w(ps_free.get(3)), inc=PE)
            tkn = None
            for mc in range(nmc):
                tkn = emit("vector", lambda e, m=mc: e.scalar_tensor_tensor(
                    dst_cols_fn(m), f32v(m * 512, 512), nw_sb[:, m:m + 1], PS_SC[:, 1536:2048],
                    OP.mult, OP.mult),
                    wait=[(PE, tbc)], inc=VE)
            ps_free[3] = (VE, tkn)

        def a_src(qi):
            if qi < 4:
                return d["xT_all"].rearrange("(fc p) t -> p fc t", p=128)[
                    :, :, qi * 512:(qi + 1) * 512]
            oq = qi - 4
            return d["xT_own"].rearrange("(fc p) t -> p fc t", p=128)[
                :, :, oq * 512:(oq + 1) * 512]

        tok_xa_ld = [None] * 6
        def a_load(qi):
            sm = SM("xa%d" % (qi % 2))
            t_ = emit("gpsimd", lambda e, a=a_src(qi), x=xab[qi % 2]: e.dma_start(
                out=x[:].rearrange("p (fc t) -> p fc t", fc=NFC), in_=a),
                wait=_w(use_xa[qi % 2]), inc=sm, step=16)
            tok_xa_ld[qi] = (sm, t_)

        a_load(0)
        a_load(1)
        slab_emit_next()
        slab_emit_next()
        for qi in range(6):
            if qi < 4:
                proj_quarter(xab[qi % 2], tok_xa_ld[qi], NCC,
                             (lambda m, c=qi: kvn[:, m * S + c * 512:m * S + c * 512 + 512]),
                             kvnw_sb, KVL, True, qi * (NCC + 1), qi)
            else:
                oq = qi - 4
                proj_quarter(xab[qi % 2], tok_xa_ld[qi], NKC_QL,
                             (lambda m, c=oq: qn[:, m * TQ + c * 512:m * TQ + c * 512 + 512]),
                             qnw_sb, QL, False, 4 * (NCC + 1) + oq * NKC_QL, 0)
            use_xa[qi % 2] = (PE, PE.n)
            if qi + 2 < 6:
                a_load(qi + 2)

        if DEBUG_SPILL:
            emit("sync", lambda e: e.dma_start(out=dbg_qn, in_=qn), wait=[(VE, VE.n)],
                 inc=SM("dbg"), step=16)
            emit("sync", lambda e: e.dma_start(out=dbg_kvn, in_=kvn), inc=SM("dbg"), step=16)
            emit("sync", lambda e: e.dma_start(out=dbg_kpe, in_=kpe), inc=SM("dbg"), step=16)

        E.barrier(allsems())

        kslab = [bfv(0, 2048), bfv(2048, 2048)]
        vw = [bfv(4096, 2048), bfv(6144, 2048)]
        qbs = [bfv(8192, 1536), bfv(9728, 1536)]
        qbr = bfv(11264, 1536)
        knope = bfv(12800, 4 * 2048)
        vbuf = bfv(20992, 16 * 512)
        qnope = [bfv(29184, 1024), bfv(30208, 1024)]
        qpe = [bfv(31232, 1024)[0:64, :], bfv(32256, 1024)[0:64, :]]
        Pt = bfv(33280, 2048)
        Pn = bfv(35328, 2048)
        PT = bfv(37376, 2048)
        attn_h = [bfv(39424, 1024), bfv(40448, 1024)]
        ropeA = f32v(0, 1024)
        ropeB = f32v(1024, 1024)
        ropeT1 = f32v(2048, 1024)
        ropeT2 = f32v(3072, 1024)
        ropeC_v = f32v(4096, 1024)
        ropeS_v = f32v(5120, 1024)

        emit("sync", lambda e: e.dma_start(out=ropeC_v, in_=d["ropeC"]), inc=SM("consts"), step=16)
        emit("sync", lambda e: e.dma_start(out=ropeS_v, in_=d["ropeS"]), inc=SM("consts"), step=16)
        tok_bconsts = (SM("consts"), SM("consts").n)

        use_group = [None, None]
        use_qbs = [None, None]
        use_qbr = None
        use_attn = [None, None]
        pend = {}
        tok_tr_pe_prev = None

        def flush_A():
            if not pend or pend.get("pn") is not None:
                return
            z, n = pend["zcol"], pend["ncols"]
            trz = emit("vector", lambda e, z_=z: e.reciprocal(
                zbuf[:, 16 + z_:17 + z_], zbuf[:, z_:z_ + 1]),
                wait=[(SE, pend["exp"])], inc=VE)
            pend["pn"] = emit("vector", lambda e, n_=n, z_=z: e.tensor_scalar_mul(
                Pn[:, 0:n_], Pt[:, 0:n_], zbuf[:, 16 + z_:17 + z_]),
                wait=_w((VE, trz), tok_tr_pe_prev), inc=VE)

        def flush_B():
            nonlocal tok_tr_pe_prev
            if not pend:
                return
            flush_A()
            s, Ks, h4, i = pend["s"], pend["Ks"], pend["h4"], pend["i"]
            nwave = (Ks + 7) // 8
            tok_tr_evac = None
            for wv_ in range(nwave):
                ptps = PS_PTA if wv_ % 2 == 0 else PS_PTB
                wlen = min(8, Ks - wv_ * 8)
                ttr = None
                for j in range(wlen):
                    tc = wv_ * 8 + j
                    ttr = emit("tensor", lambda e, p=ptps, jj=j, t=tc:
                               e.transpose(p[:, jj * 128:(jj + 1) * 128],
                                           Pn[:, t * 128:(t + 1) * 128], ident_sb[:]),
                               wait=(_w((VE, pend["pn"]), tok_pt_evac[wv_ % 2])
                                     if j == 0 else ()),
                               inc=(PE if j == wlen - 1 else None))
                tev = emit("vector", lambda e, p=ptps, w_=wv_, l=wlen: e.tensor_copy(
                    PT[:, w_ * 1024:w_ * 1024 + l * 128], p[:, 0:l * 128]),
                    wait=[(PE, ttr)], inc=VE)
                tok_pt_evac[wv_ % 2] = (VE, tev)
                tok_tr_evac = (VE, tev)
            tok_tr_pe_prev = (PE, PE.n)
            psv_i = s % 2
            psv = PS_A if psv_i == 0 else PS_B
            tmm = None
            for tc in range(Ks):
                tmm = emit("tensor", lambda e, p=psv, t=tc, h4_=h4,
                           st=(tc == 0), sp=(tc == Ks - 1):
                           e.matmul(p[:, 0:128],
                                    vbuf[:, t * 512 + h4_ * 128:t * 512 + (h4_ + 1) * 128],
                                    PT[:, t * 128:(t + 1) * 128], start=st, stop=sp),
                           wait=(_w(tok_tr_evac, tok_av_evac[psv_i], tok_v_ready)
                                 if tc == 0 else ()),
                           inc=(PE if tc == Ks - 1 else None))
            tev = emit("scalar", lambda e, p=psv, ss=s, a=pend["attn"]: e.copy(
                a[:, ss * 128:(ss + 1) * 128], p[:, 0:128]),
                wait=[(PE, tmm)] + (_w(use_attn[i]) if s == 0 else []),
                inc=SE)
            tok_av_evac[psv_i] = (SE, tev)
            if s == NSLOT - 1:
                sm = SM("ast%d" % i)
                tst = emit("sync", lambda e, a=pend["attn"], h=pend["hh"]: e.dma_start(
                    out=attn_sc[h * 128:(h + 1) * 128, :], in_=a),
                    wait=[(SE, SE.n)], inc=sm, step=16)
                use_attn[i] = (sm, tst)
            pend.clear()
        tok_av_evac = {0: None, 1: None}
        tok_pt_evac = {0: None, 1: None}
        tok_knope_ready = None
        tok_v_ready = None
        tok_scores_free = None
        tok_qn_evac = [None, None]
        tok_rope_done = None

        for pair in range(16):
            he, ho = 2 * pair, 2 * pair + 1
            if pair % 2 == 0:
                flush_B()
                g4 = pair // 2
                sl = g4 % 2
                smk, smv = SM("kv%d" % sl), SM("vw%d" % sl)
                tok_k = (smk, emit("gpsimd", lambda e, s_=kslab[sl], g=g4: e.dma_start(
                    out=s_.rearrange("p (h cc d) -> p h cc d", h=4, cc=NCC),
                    in_=d["wkvb_k"][4 * g:4 * g + 4].rearrange("h p cc d -> p h cc d")),
                    wait=_w(use_group[sl]), inc=smk, step=16))
                tok_vw = (smv, emit("gpsimd", lambda e, s_=vw[sl], g=g4: e.dma_start(
                    out=s_.rearrange("p (cc hd) -> p cc hd", cc=NCC),
                    in_=d["wkvb_v"][:, :, 4 * g:4 * g + 4, :].rearrange("cc p h d -> p cc (h d)")),
                    wait=_w(use_group[sl]), inc=smv, step=16))
                wv = _w(use_group[0], use_group[1])
                for tc in range(16):
                    psv = PS_A if tc % 2 == 0 else PS_B
                    wk = _w(tok_vw, tok_av_evac[tc % 2]) + (wv if tc < 2 else [])
                    tmm = None
                    for cc in range(NCC):
                        tmm = emit("tensor", lambda e, p=psv, c=cc, t=tc, s_=vw[sl],
                                   st=(cc == 0), sp=(cc == NCC - 1):
                                   e.matmul(p[:, 0:512], kvn[:, c * S + t * 128:c * S + t * 128 + 128],
                                            s_[:, c * 512:(c + 1) * 512], start=st, stop=sp),
                                   wait=(wk if cc == 0 else ()),
                                   inc=(PE if cc == NCC - 1 else None))
                    tev = emit("vector", lambda e, p=psv, t=tc: e.tensor_copy(
                        vbuf[:, t * 512:(t + 1) * 512], p[:, 0:512]),
                        wait=[(PE, tmm)], inc=VE)
                    tok_av_evac[tc % 2] = (VE, tev)
                tok_v_ready = (VE, VE.n)
                for hh4 in range(4):
                    for nb in range(4):
                        psv = PS_A if nb % 2 == 0 else PS_B
                        wk = _w(tok_k, tok_av_evac[nb % 2])
                        tmm = None
                        for cc in range(NCC):
                            tmm = emit("tensor", lambda e, p=psv, h=hh4, c=cc, n=nb, s_=kslab[sl],
                                       st=(cc == 0), sp=(cc == NCC - 1):
                                       e.matmul(p[:, 0:512],
                                                s_[:, (h * NCC + c) * 128:(h * NCC + c) * 128 + 128],
                                                kvn[:, c * S + n * 512:c * S + n * 512 + 512],
                                                start=st, stop=sp),
                                       wait=(wk if cc == 0 else ()),
                                       inc=(PE if cc == NCC - 1 else None))
                        tev = emit("vector", lambda e, p=psv, h=hh4, n=nb: e.tensor_copy(
                            knope[:, h * 2048 + n * 512:h * 2048 + n * 512 + 512], p[:, 0:512]),
                            wait=[(PE, tmm)], inc=VE)
                        tok_av_evac[nb % 2] = (VE, tev)
                tok_knope_ready = (VE, VE.n)

            tok_qbs_ld = [None, None]
            for i, hh in enumerate((he, ho)):
                sm = SM("qbs%d" % i)
                tok_qbs_ld[i] = (sm, emit("gpsimd", lambda e, s_=qbs[i], h=hh: e.dma_start(
                    out=s_.rearrange("p (kc m) -> p kc m", kc=NKC_QL), in_=d["wqb_t"][h]),
                    wait=_w(use_qbs[i]), inc=sm, step=16))
            smr_ = SM("qbr")
            tok_qbr_ld = (smr_, emit("gpsimd", lambda e, pr=pair: e.dma_start(
                out=qbr.rearrange("p (kc m) -> p kc m", kc=NKC_QL), in_=d["wqbr_t"][pr]),
                wait=_w(use_qbr), inc=smr_, step=16))

            for i, hh in enumerate((he, ho)):
                wq = _w(tok_qbs_ld[i], tok_scores_free,
                        tok_qn_evac[0], tok_qn_evac[1])
                tmm = None
                for nb in range(2):
                    for kc in range(NKC_QL):
                        tmm = emit("tensor", lambda e, n=nb, k=kc, s_=qbs[i],
                                   st=(kc == 0), sp=(kc == NKC_QL - 1):
                                   e.matmul(PS_SC[:, n * 512:(n + 1) * 512],
                                            s_[:, k * 128:(k + 1) * 128],
                                            qn[:, k * TQ + n * 512:k * TQ + n * 512 + 512],
                                            start=st, stop=sp),
                                   wait=(wq if (nb == 0 and kc == 0) else ()),
                                   inc=(PE if (nb == 1 and kc == NKC_QL - 1) else None))
                tev = emit("scalar", lambda e, o=qnope[i]: e.activation(
                    o, PS_SC[:, 0:1024], AF.Copy, scale=SCALE),
                    wait=[(PE, tmm)], inc=SE)
                tok_qn_evac[i] = (SE, tev)
                use_qbs[i] = (PE, tmm)
            wr = _w(tok_qbr_ld, tok_rope_done, tok_scores_free)
            tmm = None
            for nb in range(2):
                for kc in range(NKC_QL):
                    tmm = emit("tensor", lambda e, n=nb, k=kc,
                               st=(kc == 0), sp=(kc == NKC_QL - 1):
                               e.matmul(PS_SC[:, 1024 + n * 512:1024 + (n + 1) * 512],
                                        qbr[:, k * 128:(k + 1) * 128],
                                        qn[:, k * TQ + n * 512:k * TQ + n * 512 + 512],
                                        start=st, stop=sp),
                               wait=(wr if (nb == 0 and kc == 0) else ()),
                               inc=(PE if (nb == 1 and kc == NKC_QL - 1) else None))
            use_qbr = (PE, tmm)
            trv = emit("vector", lambda e: e.tensor_copy(
                ropeA[:, :], PS_SC[:, 1024:2048]),
                wait=[(PE, tmm)], inc=VE)
            tok_rope_done = (VE, trv)
            tb = None
            for (db, sb_) in [(0, 32), (32, 0), (64, 96), (96, 64)]:
                tb = emit("scalar", lambda e, a=db, b_=sb_: e.copy(
                    ropeB[a:a + 32, :], ropeA[b_:b_ + 32, :]),
                    wait=[(VE, tok_rope_done[1])], inc=SE)
            emit("vector", lambda e: e.tensor_mul(ropeT1[:], ropeA[:], ropeC_v[:]),
                 wait=_w((SE, tb), tok_bconsts), inc=VE)
            emit("vector", lambda e: e.tensor_mul(ropeT2[:], ropeB[:], ropeS_v[:]), inc=VE)
            emit("vector", lambda e: e.tensor_add(qpe[0][:, :], ropeT1[0:64, :], ropeT2[0:64, :]),
                 inc=VE)
            tq2 = emit("vector", lambda e: e.tensor_add(
                qpe[1][:, :], ropeT1[64:128, :], ropeT2[64:128, :]), inc=VE)
            tok_qpe = (VE, tq2)

            for i, hh in enumerate((he, ho)):
                h4 = hh % 4
                for s in range(NSLOT):
                    Ks = KS[s]
                    ncols = Ks * 128
                    nbk = (ncols + 511) // 512
                    wsc = _w(tok_scores_free, tok_qn_evac[i], tok_qpe, tok_knope_ready,
                             tok_consts)
                    first = True
                    for nb in range(nbk):
                        c0 = nb * 512
                        c1 = min(ncols, c0 + 512)
                        emit("tensor", lambda e, a=c0, b_=c1, q=qnope[i], h4_=h4, ss=s:
                             e.matmul(PS_SC[:, a:b_], q[:, ss * 128:(ss + 1) * 128],
                                      knope[:, h4_ * 2048 + a:h4_ * 2048 + b_],
                                      start=True, stop=False),
                             wait=(wsc if first else ()))
                        first = False
                        emit("tensor", lambda e, a=c0, b_=c1, qp=qpe[i], ss=s,
                             sp=(c1 < ncols):
                             e.matmul(PS_SC[:, a:b_], qp[:, ss * 128:(ss + 1) * 128],
                                      kpe[:, a:b_], start=False, stop=sp))
                    tok_sc = None
                    for j in range(2):
                        c0 = (Ks - 2 + j) * 128
                        tok_sc = emit("tensor", lambda e, a=c0, ss=s, jj=j, sp=(j == 1):
                                      e.matmul(PS_SC[:, a:a + 128], ident_sb[:],
                                               mask2_sb[:, (ss * 2 + jj) * 128:(ss * 2 + jj + 1) * 128],
                                               start=False, stop=sp),
                                      inc=(PE if j == 1 else None))
                    flush_A()
                    zcol = i * 8 + s
                    tmax = emit("vector", lambda e, n=ncols, z=zcol: e.reduce_max(
                        zbuf[:, 32 + z:33 + z], PS_SC[:, 0:n], AX.X, negate=True),
                        wait=_w((PE, tok_sc), tok_scores_free), inc=VE)
                    texp = emit("scalar", lambda e, n=ncols, z=zcol: e.activation(
                        Pt[:, 0:n], PS_SC[:, 0:n], AF.Exp, bias=zbuf[:, 32 + z:33 + z],
                        accum_out=zbuf[:, z:z + 1]),
                        wait=[(VE, tmax)], inc=SE)
                    tok_scores_free = (SE, texp)
                    flush_B()
                    pend.update(i=i, hh=hh, h4=h4, s=s, Ks=Ks, ncols=ncols, zcol=zcol,
                                exp=texp, pn=None, attn=attn_h[i])
            use_group[(ho // 4) % 2] = (PE, PE.n)
        flush_B()
        tok_bend_pe = (PE, PE.n)
        tok_bend_ve = (VE, VE.n)
        tok_bend_se = (SE, SE.n)
        tok_ast = [(SM("ast0"), SM("ast0").n), (SM("ast1"), SM("ast1").n)]

        rhsb = [bfv(0, 16384), bfv(16384, 16384)]
        wos = [bfv(32768, 4096), bfv(36864, 4096)]
        xbf_st = [bfv(40960, 512), bfv(41472, 512)]
        sq_st = [bfv(41984, 512), bfv(42496, 512)]
        p2yb = [bfv(43008, 512), bfv(43520, 512), bfv(44544, 512)]
        x32b = [f32v(0, 512), f32v(512, 512)]
        xres_st = [f32v(1024, 512), f32v(1536, 512)]
        p2x = [f32v(2048, 512), f32v(2560, 512), f32v(3072, 512)]
        p2t = [f32v(3584, 512), f32v(4096, 512), f32v(4608, 512)]

        use_wos = [None, None]
        use_x32 = [None, None]
        use_xres = [None, None]
        use_ld = [None, None, None]
        use_y = [None, None, None]
        use_ysc = [None, None, None]
        use_tyb = [None, None, None]
        tok_wo_free = [None, None]
        tok_stat_pe = [None, None]
        tok_rhs_t = [None, None]
        for c_ in range(2):
            smr = SM("rhs%d" % c_)
            for g in range(2):
                emit("sync", lambda e, gg=g, c=c_: e.dma_start(
                    out=rhsb[c][:, gg * 8192:(gg + 1) * 8192].rearrange("p (kc t) -> p kc t", kc=16),
                    in_=attn_sc.rearrange("(kc p) t -> p kc t", p=128)[
                        :, gg * 16:(gg + 1) * 16, c * 512:(c + 1) * 512]),
                    wait=(_w(tok_bend_pe, tok_bend_ve, *tok_ast) if g == 0 else ()),
                    inc=smr, step=16)
            tok_rhs_t[c_] = (smr, smr.n)
        for ct in range(2):
            rhs = rhsb[ct]
            tok_rhs = tok_rhs_t[ct]

            pend_stats = None
            for fc in range(NFC):
                sl = fc % 2
                smw, smx = SM("wo%d" % sl), SM("x32%d" % sl)
                first2 = (ct == 0 and fc < 2)
                tok_ws = (smw, emit("gpsimd", lambda e, s_=wos[sl], f=fc: e.dma_start(
                    out=s_.rearrange("p (kc m) -> p kc m", kc=NFC), in_=d["wo_t"][f]),
                    wait=_w(use_wos[sl]) + (_w(tok_bend_pe, tok_bend_ve, *tok_ast)
                                            if first2 else []),
                    inc=smw, step=16))
                tok_x32 = (smx, emit("sync", lambda e, b_=x32b[sl], f=fc, c=ct: e.dma_start(
                    out=b_, in_=d["xT_own"][f * 128:(f + 1) * 128, c * 512:(c + 1) * 512]),
                    wait=_w(use_x32[sl]) + (_w(tok_bend_ve, tok_bend_se) if first2 else []),
                    inc=smx, step=16))
                ps = PS_SC[:, sl * 512:(sl + 1) * 512]
                wmm = _w(tok_ws, tok_rhs, tok_wo_free[sl]) + (
                    _w(tok_scores_free) if first2 else [])
                tmm = None
                for kc in range(NFC):
                    tmm = emit("tensor", lambda e, p=ps, k=kc, s_=wos[sl], r_=rhs,
                               st=(kc == 0), sp=(kc == NFC - 1):
                               e.matmul(p, s_[:, k * 128:(k + 1) * 128],
                                        r_[:, k * 512:(k + 1) * 512], start=st, stop=sp),
                               wait=(wmm if kc == 0 else ()),
                               inc=(PE if kc == NFC - 1 else None))
                use_wos[sl] = (PE, tmm)
                if pend_stats is not None:
                    psl, ptbf, ptsq, pfc = pend_stats
                    wst = _w(tok_bend_se) if (ct == 0 and pfc == 0) else []
                    emit("tensor", lambda e, l_=psl, st=(pfc == 0): e.matmul(
                        PS_A[0:1, :], onesb_sb[:], xbf_st[l_][:], start=st, stop=False),
                        wait=[(VE, ptbf)] + wst)
                    tss = emit("tensor", lambda e, l_=psl, st=(pfc == 0): e.matmul(
                        PS_B[0:1, :], onesb_sb[:], sq_st[l_][:], start=st, stop=False),
                        wait=[(SE, ptsq)] + wst, inc=PE)
                    tok_stat_pe[psl] = (PE, tss)
                tres = emit("vector", lambda e, o=xres_st[sl], p=ps, b_=x32b[sl]: e.tensor_add(
                    o, p, b_),
                    wait=_w((PE, tmm), tok_x32, use_xres[sl]), inc=VE)
                tok_wo_free[sl] = (VE, tres)
                use_x32[sl] = (VE, tres)
                tbf = emit("vector", lambda e, i=xres_st[sl], o=xbf_st[sl]: e.tensor_copy(o[:], i),
                           wait=_w(tok_stat_pe[sl]) + (_w(*tok_ast) if first2 else []),
                           inc=VE)
                tsq = emit("scalar", lambda e, i=xres_st[sl], o=sq_st[sl]: e.activation(
                    o[:], i, AF.Square),
                    wait=_w((VE, tres), tok_stat_pe[sl]), inc=SE)
                pend_stats = (sl, tbf, tsq, fc)
                smxs = SM("xst%d" % sl)
                tstx = emit("sync", lambda e, i=xres_st[sl], f=fc, c=ct: e.dma_start(
                    out=xres_sc[f * 128:(f + 1) * 128, c * 512:(c + 1) * 512], in_=i),
                    wait=[(VE, tres)], inc=smxs, step=16)
                if DEBUG_SPILL:
                    tstx = emit("sync", lambda e, i=xres_st[sl], f=fc, c=ct: e.dma_start(
                        out=dbg_x1[f * 128:(f + 1) * 128, c * 512:(c + 1) * 512], in_=i),
                        inc=smxs, step=16)
                use_xres[sl] = (smxs, tstx)
            psl, ptbf, ptsq, pfc = pend_stats
            emit("tensor", lambda e, l_=psl: e.matmul(PS_A[0:1, :], onesb_sb[:], xbf_st[l_][:],
                                                      start=False, stop=True),
                 wait=[(VE, ptbf)])
            emit("tensor", lambda e, l_=psl: e.matmul(PS_B[0:1, :], onesb_sb[:], sq_st[l_][:],
                                                      start=False, stop=True),
                 wait=[(SE, ptsq)], inc=PE)
            tok_stats = PE.n

            emit("vector", lambda e: e.tensor_scalar(rowt(0, 512), PS_A[0:1, :],
                                                     1.0 / H, None, OP.mult),
                 wait=[(PE, tok_stats)], inc=VE)
            emit("vector", lambda e: e.tensor_scalar(rowt(1, 512), PS_B[0:1, :],
                                                     1.0 / H, None, OP.mult), inc=VE)
            emit("vector", lambda e: e.tensor_mul(rowt(2, 512), rowt(0, 512), rowt(0, 512)), inc=VE)
            emit("vector", lambda e: e.tensor_sub(rowt(3, 512), rowt(1, 512), rowt(2, 512)), inc=VE)
            emit("vector", lambda e: e.tensor_scalar(rowt(1, 512), rowt(3, 512), EPS_LN,
                                                     None, OP.add), inc=VE)
            t8 = emit("vector", lambda e: e.reciprocal(rowt(2, 512), rowt(1, 512)), inc=VE)
            t9 = emit("scalar", lambda e: e.activation(rowt(1, 512), rowt(2, 512), AF.Sqrt),
                      wait=[(VE, t8)], inc=SE)
            t10 = emit("vector", lambda e: e.tensor_mul(rowt(2, 512), rowt(0, 512), rowt(1, 512)),
                       wait=[(SE, t9)], inc=VE)
            wbc = _w(tok_wo_free[0], tok_wo_free[1], (VE, t10))
            rb_ap = PS_SC[:, 1024:1536] if ct == 0 else PS_A[:, :]
            mrb_ap = PS_SC[:, 1536:2048] if ct == 0 else PS_B[:, :]
            emit("tensor", lambda e, a=rb_ap: e.matmul(a, ones1_sb[:], rowt(1, 512),
                                                       start=True, stop=True), wait=wbc)
            emit("tensor", lambda e, a=mrb_ap: e.matmul(a, ones1_sb[:], rowt(2, 512),
                                                        start=True, stop=True), inc=PE)
            tok_bc = PE.n

            for fc in range(NFC):
                par = fc % 3
                sl = fc % 2
                smld = SM("xld%d" % par)
                tld = (smld, emit("sync", lambda e, o=p2x[par], f=fc, c=ct: e.dma_start(
                    out=o, in_=xres_sc[f * 128:(f + 1) * 128, c * 512:(c + 1) * 512]),
                    wait=_w(use_ld[par], use_xres[sl], use_ysc[par], use_tyb[par]),
                    inc=smld, step=16))
                tt1 = emit("vector", lambda e, i=p2x[par], o=p2t[par], a=rb_ap: e.tensor_mul(
                    o, i, a),
                    wait=_w(tld, (PE, tok_bc)), inc=VE)
                ty = emit("vector", lambda e, o=p2x[par], i=p2t[par], a=mrb_ap: e.tensor_sub(
                    o, i, a), wait=_w(use_ysc[par], use_y[par]), inc=VE)
                use_ld[par] = (VE, ty)
                use_xres[sl] = tld
                tyb = emit("scalar", lambda e, i=p2x[par], o=p2yb[par]: e.copy(o, i),
                           wait=_w((VE, ty), use_y[par]), inc=SE)
                use_tyb[par] = (SE, tyb)
                smy, smyb = SM("yst%d" % par), SM("ybst%d" % par)
                tsy = emit("scalar", lambda e, i=p2x[par], f=fc, c=ct: e.dma_start(
                    out=y_sc[f * 128:(f + 1) * 128, c * 512:(c + 1) * 512], in_=i),
                    wait=[(VE, ty)], inc=smy, step=16)
                ts2 = emit("scalar", lambda e, i=p2yb[par], f=fc, c=ct: e.dma_start(
                    out=ybf_sc[f * 128:(f + 1) * 128, c * 512:(c + 1) * 512], in_=i),
                    wait=[(SE, tyb)], inc=smyb, step=16)
                use_y[par] = (smyb, ts2)
                use_ysc[par] = (smy, tsy)
            if ct == 0:
                tok_c_p2t0 = (VE, VE.n)
                y_done0 = [(SM("ybst%d" % i2), SM("ybst%d" % i2).n) for i2 in range(3)]
        y_done1 = [(SM("ybst%d" % i2), SM("ybst%d" % i2).n) for i2 in range(3)]
        tok_c_pass2 = (VE, VE.n)
        tok_c_pe = (PE, PE.n)

        y_tile = bfv(0, 16384)
        h_buf = bfv(16384, 44032)
        gu = [bfv(60416, 2048), bfv(62464, 2048), bfv(64512, 2048), bfv(66560, 2048)]
        tmp_sg = bfv(68608, 512)
        tmp_us = bfv(69120, 512)
        dslab = [bfv(60416, 5504), bfv(65920, 5504)]
        xbf2 = [bfv(71424, 512), bfv(71936, 512)]
        sq2 = [bfv(72448, 512), bfv(72960, 512)]
        y32b = [f32v(0, 512), f32v(512, 512)]
        xr2 = [f32v(1024, 512), f32v(1536, 512)]
        d2x = [f32v(2048, 512), f32v(2560, 512), f32v(3072, 512)]
        d2t = [f32v(3584, 512), f32v(4096, 512), f32v(4608, 512)]

        tok_ytile_user = None
        use_y32 = [None, None]
        use_xr2 = [None, None]
        use_ds = [None, None]
        prev_q = {0: tok_wo_free[0], 1: tok_wo_free[1], 2: tok_c_p2t0, 3: tok_c_p2t0}
        tok_d_p2 = None
        use_ld2 = [None, None, None]
        use_out = [None, None, None]
        for tt in range(2):
            w0 = _w(*(y_done0 if tt == 0 else y_done1)) + _w(
                tok_ytile_user, (tok_c_pe if tt == 0 else None))
            for g in range(4):
                emit("gpsimd", lambda e, gg=g, t_=tt: e.dma_start(
                    out=y_tile[:, gg * 4096:(gg + 1) * 4096].rearrange("p (kc t) -> p kc t", kc=8),
                    in_=ybf_sc.rearrange("(kc p) t -> p kc t", p=128)[:, gg * 8:(gg + 1) * 8,
                                                                     t_ * 512:(t_ + 1) * 512]),
                    wait=(w0 if g == 0 else ()), inc=SM("ytile"), step=16)
            tok_ytile = (SM("ytile"), SM("ytile").n)

            use_gu = [None, None, None, None]
            tok_guevac = {}
            tok_th_prev = None
            for ic in range(NIC):
                sl = ic % 2
                toks_ld = [None] * 4
                for m in range(2):
                    for half in range(2):
                        bi = m * 2 + half
                        src = d["wg_t"] if m == 0 else d["wu_t"]
                        sm = SM("gu%d" % bi)
                        toks_ld[bi] = (sm, emit("gpsimd", lambda e, s_=gu[bi], a=src, i=ic, hf=half:
                                                e.dma_start(
                                                    out=s_.rearrange("p (kc m_) -> p kc m_", kc=16),
                                                    in_=a[i, :, hf * 16:(hf + 1) * 16, :]),
                                                wait=_w(use_gu[bi])
                                                + (_w(use_ds[0], use_ds[1],
                                                      (tok_bend_pe if tt == 0 else None))
                                                   if ic == 0 else []),
                                                inc=sm, step=16))
                qg = PS_SC[:, (2 * sl) * 512:(2 * sl + 1) * 512]
                qu = PS_SC[:, (2 * sl + 1) * 512:(2 * sl + 2) * 512]
                tok_g = tok_u = None
                for m in range(2):
                    ps = qg if m == 0 else qu
                    tmm = None
                    for half in range(2):
                        bi = m * 2 + half
                        for k in range(16):
                            kc = half * 16 + k
                            tmm = emit("tensor", lambda e, p=ps, s_=gu[bi], kk=k, c=kc,
                                       st=(kc == 0), sp=(kc == NFC - 1):
                                       e.matmul(p, s_[:, kk * 128:(kk + 1) * 128],
                                                y_tile[:, c * 512:(c + 1) * 512],
                                                start=st, stop=sp),
                                       wait=(_w(toks_ld[bi], tok_ytile,
                                                tok_guevac.get((sl, m)) or prev_q[2 * sl + m])
                                             if kc == half * 16 else ()),
                                       inc=(PE if k == 15 else None))
                        use_gu[bi] = (PE, tmm)
                    if m == 0:
                        tok_g = tmm
                    else:
                        tok_u = tmm
                tsg = emit("scalar", lambda e, p=qg, i=ic: e.activation(
                    tmp_sg[:], p, AF.Silu, scale=sg_sb[:, i:i + 1]),
                    wait=_w((PE, tok_g), tok_th_prev), inc=SE)
                tus = emit("vector", lambda e, p=qu, i=ic: e.tensor_scalar_mul(
                    tmp_us[:], p, su_sb[:, i:i + 1]),
                    wait=[(PE, tok_u)], inc=VE)
                if DEBUG_SPILL and ic == 0:
                    tdbg1 = emit("sync", lambda e, t_=tt: e.dma_start(out=dbg_sg[t_], in_=tmp_sg[:]),
                                 wait=[(SE, tsg)], inc=SM("dbg"), step=16)
                    tdbg2 = emit("sync", lambda e, t_=tt: e.dma_start(out=dbg_us[t_], in_=tmp_us[:]),
                                 wait=[(VE, tus)], inc=SM("dbg"), step=16)
                    tok_th_prev = (SM("dbg"), tdbg2)
                th = emit("vector", lambda e, i=ic: e.tensor_mul(
                    h_buf[:, i * 512:(i + 1) * 512], tmp_sg[:], tmp_us[:]),
                    wait=[(SE, tsg), (VE, tus)] + (_w(tok_ytile_user) if ic == 0 else [])
                    + (_w(tok_th_prev) if (DEBUG_SPILL and ic == 0) else []),
                    inc=VE)
                tok_th_prev = (VE, th)
                tok_guevac[(sl, 0)] = (SE, tsg)
                tok_guevac[(sl, 1)] = (VE, tus)
            tok_h_done = (VE, VE.n)
            if DEBUG_SPILL:
                emit("sync", lambda e, t_=tt: e.dma_start(out=dbg_h[t_], in_=h_buf),
                     wait=[tok_h_done], inc=SM("dbg"), step=16)

            tok_dfree = {}
            tok_stats2 = None
            pend_stats2 = None
            tok_stat_pe = [None, None]
            for fc in range(NFC):
                sl = fc % 2
                toks_ld = [None, None]
                for half in range(2):
                    sm = SM("ds%d" % half)
                    toks_ld[half] = (sm, emit("gpsimd", lambda e, s_=dslab[half], f=fc, hf=half:
                                              e.dma_start(
                                                  out=s_.rearrange("p (kc m_) -> p kc m_", kc=43),
                                                  in_=d["wd_t"][f, :, hf * 43:(hf + 1) * 43, :]),
                                              wait=_w(use_ds[half])
                                              + (_w(use_gu[0], use_gu[1], use_gu[2], use_gu[3],
                                                    tok_h_done) if fc == 0 else []),
                                              inc=sm, step=16))
                sm32 = SM("y32%d" % sl)
                ty32 = (sm32, emit("sync", lambda e, o=y32b[sl], f=fc, t_=tt: e.dma_start(
                    out=o, in_=y_sc[f * 128:(f + 1) * 128, t_ * 512:(t_ + 1) * 512]),
                    wait=_w(use_y32[sl])
                    + (_w(tok_c_pass2) if (tt == 0 and fc < 2) else []),
                    inc=sm32, step=16))
                ps = PS_SC[:, sl * 512:(sl + 1) * 512]
                tmm = None
                for half in range(2):
                    for k in range(43):
                        kc = half * 43 + k
                        tmm = emit("tensor", lambda e, p=ps, s_=dslab[half], kk=k, c=kc,
                                   st=(kc == 0), sp=(kc == NIC - 1):
                                   e.matmul(p, s_[:, kk * 128:(kk + 1) * 128],
                                            h_buf[:, c * 512:(c + 1) * 512], start=st, stop=sp),
                                   wait=(_w(toks_ld[half], tok_dfree.get(sl),
                                            tok_h_done) if kc == half * 43 else ()),
                                   inc=(PE if k == 42 else None))
                    use_ds[half] = (PE, tmm)
                if pend_stats2 is not None:
                    psl, ptbf, ptsq, pfc = pend_stats2
                    wst = (_w(tok_c_pass2 if tt == 0 else tok_d_p2)
                           if pfc == 0 else [])
                    emit("tensor", lambda e, l_=psl, st=(pfc == 0): e.matmul(
                        PS_A[0:1, :], onesb_sb[:], xbf2[l_][:], start=st, stop=False),
                        wait=[(VE, ptbf)] + wst)
                    tss = emit("tensor", lambda e, l_=psl, st=(pfc == 0): e.matmul(
                        PS_B[0:1, :], onesb_sb[:], sq2[l_][:], start=st, stop=False),
                        wait=[(SE, ptsq)] + wst, inc=PE)
                    tok_stat_pe[psl] = (PE, tss)
                txr = emit("vector", lambda e, o=xr2[sl], p=ps, f=fc, b_=y32b[sl]:
                           e.scalar_tensor_tensor(o, p, sd_sb[:, f:f + 1], b_, OP.mult, OP.add),
                           wait=_w((PE, tmm), ty32, use_xr2[sl]), inc=VE)
                tok_dfree[sl] = (VE, txr)
                use_y32[sl] = (VE, txr)
                tbf = emit("vector", lambda e, i=xr2[sl], o=xbf2[sl]: e.tensor_copy(o[:], i),
                           wait=_w(tok_stat_pe[sl]), inc=VE)
                tsq = emit("scalar", lambda e, i=xr2[sl], o=sq2[sl]: e.activation(o[:], i, AF.Square),
                           wait=_w((VE, txr), tok_stat_pe[sl]), inc=SE)
                pend_stats2 = (sl, tbf, tsq, fc)
                smxs = SM("xst%d" % sl)
                tsx = emit("sync", lambda e, i=xr2[sl], f=fc, t_=tt: e.dma_start(
                    out=xres_sc[f * 128:(f + 1) * 128, t_ * 512:(t_ + 1) * 512], in_=i),
                    wait=[(VE, txr)], inc=smxs, step=16)
                use_xr2[sl] = (smxs, tsx)
            psl, ptbf, ptsq, pfc = pend_stats2
            emit("tensor", lambda e, l_=psl: e.matmul(PS_A[0:1, :], onesb_sb[:], xbf2[l_][:],
                                                      start=False, stop=True),
                 wait=[(VE, ptbf)])
            emit("tensor", lambda e, l_=psl: e.matmul(PS_B[0:1, :], onesb_sb[:], sq2[l_][:],
                                                      start=False, stop=True),
                 wait=[(SE, ptsq)], inc=PE)
            tok_stats2 = (PE, PE.n)

            emit("vector", lambda e: e.tensor_scalar(rowt(0, 512), PS_A[0:1, :], 1.0 / H,
                                                     None, OP.mult),
                 wait=[(PE, PE.n)], inc=VE)
            emit("vector", lambda e: e.tensor_scalar(rowt(1, 512), PS_B[0:1, :], 1.0 / H,
                                                     None, OP.mult), inc=VE)
            emit("vector", lambda e: e.tensor_mul(rowt(2, 512), rowt(0, 512), rowt(0, 512)), inc=VE)
            emit("vector", lambda e: e.tensor_sub(rowt(3, 512), rowt(1, 512), rowt(2, 512)), inc=VE)
            emit("vector", lambda e: e.tensor_scalar(rowt(1, 512), rowt(3, 512), EPS_LN,
                                                     None, OP.add), inc=VE)
            t8 = emit("vector", lambda e: e.reciprocal(rowt(2, 512), rowt(1, 512)), inc=VE)
            t9 = emit("scalar", lambda e: e.activation(rowt(1, 512), rowt(2, 512), AF.Sqrt),
                      wait=[(VE, t8)], inc=SE)
            t10 = emit("vector", lambda e: e.tensor_mul(rowt(2, 512), rowt(0, 512), rowt(1, 512)),
                       wait=[(SE, t9)], inc=VE)
            wbc = _w(tok_dfree.get(0), tok_dfree.get(1), (VE, t10))
            emit("tensor", lambda e: e.matmul(PS_A[:, :], ones1_sb[:], rowt(1, 512),
                                              start=True, stop=True), wait=wbc)
            emit("tensor", lambda e: e.matmul(PS_B[:, :], ones1_sb[:], rowt(2, 512),
                                              start=True, stop=True), inc=PE)
            tok_bc2 = PE.n

            for fc in range(NFC):
                par = fc % 3
                sl = fc % 2
                smld = SM("xld%d" % par)
                tld = (smld, emit("sync", lambda e, o=d2x[par], f=fc, t_=tt: e.dma_start(
                    out=o, in_=xres_sc[f * 128:(f + 1) * 128, t_ * 512:(t_ + 1) * 512]),
                    wait=_w(use_ld2[par], use_xr2[sl], use_out[par]), inc=smld, step=16))
                tt1 = emit("vector", lambda e, i=d2x[par], o=d2t[par]: e.tensor_mul(
                    o, i, PS_A[:, :]),
                    wait=_w(tld, (PE, tok_bc2)), inc=VE)
                tout = emit("vector", lambda e, o=d2x[par], i=d2t[par]: e.tensor_sub(
                    o, i, PS_B[:, :]), wait=_w(use_out[par]), inc=VE)
                use_ld2[par] = (VE, tout)
                use_xr2[sl] = tld
                smo = SM("ost%d" % par)
                tso = emit("scalar", lambda e, i=d2x[par], f=fc, t_=tt: e.dma_start(
                    out=outT[f * 128:(f + 1) * 128, t_ * 512:(t_ + 1) * 512], in_=i),
                    wait=[(VE, tout)], inc=smo, step=16)
                use_out[par] = (smo, tso)
            tok_ytile_user = (PE, PE.n)
            prev_q = {0: tok_dfree.get(0), 1: tok_dfree.get(1),
                      2: tok_guevac.get((1, 0)), 3: tok_guevac.get((1, 1))}
            tok_d_p2 = (VE, VE.n)

        E.ops["sync"].append((None, [(SM("ost%d" % i2), SM("ost%d" % i2).n)
                                     for i2 in range(3)], None))

        @block.tensor
        def _(t):
            E.replay("tensor", t)

        @block.vector
        def _(v):
            E.replay("vector", v)

        @block.scalar
        def _(s):
            E.replay("scalar", s)

        @block.gpsimd
        def _(g):
            E.replay("gpsimd", g)

        @block.sync
        def _(sy):
            E.replay("sync", sy)

    return nc


_PERM_ROPE = [2 * j for j in range(32)] + [2 * j + 1 for j in range(32)]


def _rope_tabs(npos):
    freqs = 1.0 / 10000.0 ** (np.arange(0, DR, 2, dtype=np.float32) / DR)
    t = np.arange(npos, dtype=np.float32)
    ang = np.outer(t, freqs)
    return np.cos(ang).astype(np.float32), np.sin(ang).astype(np.float32)


def _prep_shared(inputs):
    wq_a = np.asarray(inputs["L0_attn_wq_a"], np.float32)
    wq_b = np.asarray(inputs["L0_attn_wq_b"], np.float32)
    wkv_a = np.asarray(inputs["L0_attn_wkv_a"], np.float32)
    wkv_b = np.asarray(inputs["L0_attn_wkv_b"], np.float32)
    wo = np.asarray(inputs["L0_attn_wo"], np.float32)
    w_gate = np.asarray(inputs["L0_ffn_w_gate"], np.int32)
    w_up = np.asarray(inputs["L0_ffn_w_up"], np.int32)
    w_down = np.asarray(inputs["L0_ffn_w_down"], np.int32)

    def tile_w(w, nmc, nkc, mw=128):
        return np.ascontiguousarray(w.reshape(nmc, mw, nkc, 128).transpose(0, 3, 2, 1))

    def to_i8(a):
        return np.ascontiguousarray(a.view(np.int8)[..., 0::4])

    sh = {}
    sh["wqa_t"] = tile_w(wq_a, NKC_QL, NFC)
    nope_rows = []
    rope_rows = []
    for h in range(NH):
        base = h * 192
        nope_rows.extend(range(base, base + 128))
        rope_rows.extend(base + 128 + p for p in _PERM_ROPE)
    sh["wqb_t"] = np.ascontiguousarray(
        wq_b[np.asarray(nope_rows)].reshape(NH, 128, NKC_QL, 128).transpose(0, 3, 2, 1))
    sh["wqbr_t"] = np.ascontiguousarray(
        wq_b[np.asarray(rope_rows)].reshape(NH // 2, 128, NKC_QL, 128).transpose(0, 3, 2, 1))
    sh["wkva_c"] = tile_w(wkv_a[:KVL], NCC, NFC)
    wkva_pe = wkv_a[KVL + np.asarray(_PERM_ROPE)]
    sh["wkva_pe"] = np.ascontiguousarray(wkva_pe.reshape(64, NFC, 128).transpose(2, 1, 0))
    kvb = wkv_b.reshape(NH, 256, NCC, 128)
    sh["wkvb_k"] = np.ascontiguousarray(kvb[:, :128].transpose(0, 3, 2, 1))
    sh["wkvb_v"] = np.ascontiguousarray(kvb[:, 128:].transpose(2, 3, 0, 1))
    sh["wo_t"] = tile_w(wo, NFC, NFC)
    sh["wg_t"] = to_i8(tile_w(w_gate, NIC, NFC))
    sh["wu_t"] = to_i8(tile_w(w_up, NIC, NFC))
    sh["wd_t"] = to_i8(tile_w(w_down, NFC, NIC))
    sh["qnw_t"] = np.ascontiguousarray(
        np.asarray(inputs["L0_attn_q_norm_w"], np.float32).reshape(NKC_QL, 128).T)
    sh["kvnw_t"] = np.ascontiguousarray(
        np.asarray(inputs["L0_attn_kv_norm_w"], np.float32).reshape(NCC, 128).T)
    sh["sg_t"] = np.ascontiguousarray(
        np.asarray(inputs["L0_ffn_s_gate"], np.float32).reshape(NIC, 128).T)
    sh["su_t"] = np.ascontiguousarray(
        np.asarray(inputs["L0_ffn_s_up"], np.float32).reshape(NIC, 128).T)
    sh["sd_t"] = np.ascontiguousarray(
        np.asarray(inputs["L0_ffn_s_down"], np.float32).reshape(NFC, 128).T)
    sh["ident"] = np.eye(128, dtype=BF16NP)
    sh["onesb"] = np.ones((128, 1), dtype=BF16NP)
    sh["ones1"] = np.ones((1, 128), dtype=np.float32)
    cosk, sink = _rope_tabs(S)
    sh["coskp"] = np.ascontiguousarray(np.concatenate([cosk.T, cosk.T], axis=0))
    sh["sinkp"] = np.ascontiguousarray(np.concatenate([-sink.T, sink.T], axis=0))
    return sh


def _prep_core(inputs, b, hpar):
    x = np.asarray(inputs["L0_attn_x"], np.float32)[b]
    xT = np.ascontiguousarray(x.T)
    qt = QTILES[hpar]
    own_cols = np.concatenate([np.arange(g * 128, (g + 1) * 128) for g in qt])
    pc = {}
    pc["xT_all"] = xT
    pc["xT_own"] = np.ascontiguousarray(xT[:, own_cols])
    m = np.zeros((128, NSLOT * 2 * 128), np.float32)
    for s in range(NSLOT):
        g = qt[s]
        for j in range(2):
            c = KS[s] - 2 + j
            keys = np.arange(c * 128, (c + 1) * 128)
            qrows = g * 128 + np.arange(128)
            m[:, (s * 2 + j) * 128:(s * 2 + j + 1) * 128] = np.where(
                keys[None, :] <= qrows[:, None], 0.0, MASKVAL)
    pc["mask2"] = m.astype(BF16NP)
    cosk, sink = _rope_tabs(S)
    cq = cosk[own_cols].T
    sq = sink[own_cols].T
    pc["ropeC"] = np.ascontiguousarray(
        (np.concatenate([cq, cq, cq, cq], axis=0) * SCALE).astype(np.float32))
    pc["ropeS"] = np.ascontiguousarray(
        (np.concatenate([-sq, sq, -sq, sq], axis=0) * SCALE).astype(np.float32))
    return pc, own_cols


def kernel(**inputs):
    nc = build_program()
    sh = _prep_shared(inputs)
    in_maps = []
    cols = []
    for core in range(8):
        b, hpar = core // 2, core % 2
        pc, own_cols = _prep_core(inputs, b, hpar)
        m = dict(sh)
        m.update(pc)
        in_maps.append(m)
        cols.append(own_cols)
    res = run_bass_kernel_spmd(nc, in_maps, core_ids=list(range(8)))
    out = np.empty((B, S, H), np.float32)
    for core in range(8):
        b = core // 2
        o = res.results[core]["outT"]
        out[b, cols[core], :] = np.ascontiguousarray(o).T
    return out
